# version 09 (final)
# baseline (speedup 1.0000x reference)
import numpy as np
import ml_dtypes
import concourse.bass as bass
import concourse.mybir as mybir
from concourse.bass_utils import run_bass_kernel_spmd

F32 = mybir.dt.float32
BF16 = mybir.dt.bfloat16
AF = mybir.ActivationFunctionType
ALU = mybir.AluOpType
AX = mybir.AxisListType

NCORE = 8
B, T, D = 16, 2048, 1024
SEQ_PER_CORE = B // NCORE
TOK = SEQ_PER_CORE * T
NTILE = TOK // 128
FFN_F = 4096
NE = 8
H, Kd, Vd = 4, 256, 512
CONV = 4
LN_EPS = 1e-5
RMS_EPS = 1e-6

ENGINES = ("tensor", "vector", "scalar", "sync", "gpsimd")
ALL_PHASES = ("ffn0", "attn0", "ffn1", "ffn2", "attn1", "ffn3")


class _Sem:
    __slots__ = ("h", "n", "name")


class PB:

    def __init__(self, nc, strict=False):
        self.nc = nc
        self.strict = strict
        self.stream = {e: [] for e in ENGINES}
        self.implied = {e: {} for e in ENGINES}

    def sem(self, name):
        s = _Sem()
        s.h = self.nc.alloc_semaphore(name)
        s.n = 0
        s.name = name
        return s

    def op(self, eng, fn, waits=(), inc=None, amt=1, wforce=()):
        ws = {}
        imp = self.implied[eng]
        for t in waits:
            if t is None:
                continue
            s, v = t
            if v <= 0 or imp.get(id(s), 0) >= v:
                continue
            k = id(s)
            ws[k] = (s, max(ws[k][1], v)) if k in ws else (s, v)
        for t in wforce:
            if t is None:
                continue
            s, v = t
            k = id(s)
            ws[k] = (s, max(ws[k][1], v)) if k in ws else (s, v)
        for k, (s, v) in ws.items():
            imp[k] = max(imp.get(k, 0), v)
        tok = None
        if inc is not None:
            inc.n += amt
            tok = (inc, inc.n)
            if amt == 1 and (self.strict is not True) and (
                    not isinstance(self.strict, (set, frozenset))
                    or eng not in self.strict):
                imp[id(inc)] = inc.n
        self.stream[eng].append((fn, list(ws.values()), (inc, amt) if inc else None))
        return tok

    def dma(self, eng, out, in_, waits=(), inc=None, slow=False):
        w = list(waits)
        if inc is not None and inc.n > 0:
            w.append((inc, inc.n))
        return self.op(eng, lambda e, o=out, i=in_: e.dma_start(
            out=o, in_=i, allow_slow_non_contiguous=slow), w, inc, amt=16)

    def emit(self, block):
        def body_for(name):
            def body(eng):
                for fn, ws, inc in self.stream[name]:
                    for s, v in ws:
                        eng.wait_ge(s.h, v)
                    ins = fn(eng)
                    if inc is not None:
                        ins.then_inc(inc[0].h, inc[1])
            return body
        block.sync(body_for("sync"))
        block.vector(body_for("vector"))
        block.scalar(body_for("scalar"))
        block.tensor(body_for("tensor"))
        block.gpsimd(body_for("gpsimd"))


def L(*toks):
    return [t for t in toks if t is not None]


def build_nc(phases=ALL_PHASES, strict=False):
    nc = bass.Bass("TRN2", target_bir_lowering=False, debug=False)

    xin = nc.dram_tensor("x", [TOK, D], F32, kind="ExternalInput").ap()
    yout = nc.dram_tensor("y", [TOK, D], F32, kind="ExternalOutput").ap()

    dr = {}
    def din(name, shape):
        dr[name] = nc.dram_tensor(name, list(shape), F32, kind="ExternalInput").ap()
    din("a0_ln1_g", [D]); din("a0_ln1_b", [D]); din("a0_ln2_g", [D]); din("a0_ln2_b", [D])
    din("a0_rms_w", [D])
    din("a0_wq", [8, 128, 1024]); din("a0_wk", [8, 128, 1024])
    din("a0_wv", [16, 128, 1024]); din("a0_wat", [128, 64])
    din("a0_cq", [1024, CONV]); din("a0_ck", [1024, CONV]); din("a0_cv", [2048, CONV])
    din("a0_wa", [D, H]); din("a0_dtb", [H]); din("a0_alog", [H]); din("a0_wb", [D, H])
    din("a0_wg", [16, 128, 1024]); din("a0_onw", [Vd])
    din("a0_wo", [8, 128, 2048]); din("a0_w2", [8, 128, 1024]); din("a0_b2", [D])
    din("f0_wr", [D, NE]); din("f0_br", [NE])
    din("f0_w1", [NE, 32, 128, 1024])
    din("f0_w2", [NE, 2, 8, 128, 2048])
    din("a1_wq", [128, 8192]); din("a1_bq", [D]); din("a1_wk", [128, 8192]); din("a1_bk", [D])
    din("a1_wv", [128, 8192]); din("a1_bv", [D]); din("a1_wo", [128, 8192]); din("a1_bo", [D])
    din("f1_cent_t", [D, NE]); din("f1_gam_t", [D, NE]); din("f1_bet_t", [D, NE])
    din("f1_w1t", [NE, 32, 128, 1024]); din("f1_b1t", [FFN_F, NE])
    din("f1_w2t", [NE, 2, 8, 128, 2048]); din("f1_b2", [NE, D])

    ident_bf = nc.inline_tensor(np.eye(128, dtype=ml_dtypes.bfloat16), name="identbf").ap()
    ones_bf = nc.inline_tensor(np.ones((1, 128), dtype=ml_dtypes.bfloat16), name="onesbf").ap()
    ones_f32 = nc.inline_tensor(np.ones((1, 128), dtype=np.float32), name="onesf32").ap()
    onescol_bf = nc.inline_tensor(np.ones((128, 1), dtype=ml_dtypes.bfloat16), name="onescolbf").ap()
    cmask_np = (10000.0 * np.triu(np.ones((128, 128), np.float32), 1)).astype(ml_dtypes.bfloat16)
    cmask_h = nc.inline_tensor(cmask_np, name="cmask").ap()
    lts_h = nc.inline_tensor(np.tril(np.ones((128, 128), ml_dtypes.bfloat16), -1), name="lts").ap()
    ident32_h = nc.inline_tensor(np.eye(128, dtype=np.float32), name="ident32").ap()
    uts_h = nc.inline_tensor(np.triu(np.ones((128, 128), ml_dtypes.bfloat16), 1), name="uts").ap()
    uti_h = nc.inline_tensor(np.triu(np.ones((128, 128), ml_dtypes.bfloat16), 0), name="uti").ap()

    A = nc.alloc_sbuf_tensor
    xres = A("xres", [128, NTILE, D], BF16)[:, :, :]
    ARENA_N = 66048
    arena = A("arena", [128, ARENA_N], BF16)[:, :]
    scr = A("scr", [128, 2560], F32)[:, :]
    sident = A("sident", [128, 128], BF16)[:, :]
    sones_bf = A("sones_bf", [1, 128], BF16)[:, :]
    sones_f32 = A("sones_f32", [1, 128], F32)[:, :]
    sonescol_bf = A("sonescol_bf", [128, 1], BF16)[:, :]
    cvec = A("cvec", [128, 8], F32)[:, :]
    scmask = A("scmask", [128, 128], BF16)[:, :]
    slts = A("slts", [128, 128], BF16)[:, :]
    sident32 = A("sident32", [128, 128], F32)[:, :]
    suts = A("suts", [128, 128], BF16)[:, :]
    suti = A("suti", [128, 128], BF16)[:, :]

    p32 = nc.alloc_psum_tensor("p32", [128, 7, 512], F32)[:, :, :]
    ptr = nc.alloc_psum_tensor("ptr", [128, 1024], BF16)[:, :]
    p7 = ptr.bitcast(F32)
    def pbank(i):
        return p32[:, i, :] if i < 7 else p7

    pb = PB(nc, strict=strict)
    sT = pb.sem("sT"); sV = pb.sem("sV"); sA = pb.sem("sA")
    sX = pb.sem("sX")
    sC = pb.sem("sC")
    sY = pb.sem("sY")
    SL = {k: pb.sem("s_" + k) for k in
          ("w1a", "w1b", "w2a", "w2b", "wr", "b2",
           "wa", "wb", "wc", "wd", "we", "wf")}
    ALL_SEMS = [sT, sV, sA, sX, sC, sY] + list(SL.values())

    st = {}
    bank_w = [None] * 8
    bank_r = [None] * 8

    def xw(t):
        return st.get(("xw", t))

    def set_xw(t, tok):
        st[("xw", t)] = tok

    def xr(t):
        return st.get(("xr", t))

    def set_xr(t, tok):
        st[("xr", t)] = tok

    t_id = pb.dma("sync", sident, ident_bf[:, :], inc=sC)
    t_ob = pb.dma("sync", sones_bf, ones_bf[:, :], inc=sC)
    t_of = pb.dma("sync", sones_f32, ones_f32[:, :], inc=sC)
    t_oc = pb.dma("sync", sonescol_bf, onescol_bf[:, :], inc=sC)
    pb.dma("sync", scmask, cmask_h[:, :], inc=sC)
    pb.dma("sync", slts, lts_h[:, :], inc=sC)
    pb.dma("sync", sident32, ident32_h[:, :], inc=sC)
    pb.dma("sync", suts, uts_h[:, :], inc=sC)
    pb.dma("sync", suti, uti_h[:, :], inc=sC)
    t_id = t_ob = t_of = t_oc = t_cm = (sC, sC.n)
    CONSTS = {"ln_eps": LN_EPS, "rms_eps": RMS_EPS, "one": 1.0}
    CIDX = {}
    for i, (k, v) in enumerate(CONSTS.items()):
        CIDX[k] = i
        pb.op("vector", lambda e, i=i, v=v: e.memset(cvec[:, i:i+1], v), inc=sV)
    t_consts = (sV, sV.n)

    def ccol(name):
        return cvec[:, CIDX[name]:CIDX[name]+1]

    xin3 = xin.rearrange("(t p) d -> p t d", p=128)
    for g in range(4):
        tkx = pb.dma("gpsimd", xres[:, g*8:(g+1)*8, :], xin3[:, g*8:(g+1)*8, :], inc=sX)
        for t_ in range(g*8, (g+1)*8):
            set_xw(t_, tkx)

    def snap():
        return [(s_, s_.n) for s_ in ALL_SEMS]

    ctile = scr[:, 0:1024]
    SB = 1024
    ROWB = 1040
    PH = 1056

    def ln_tile(src_ap, eps_col, garb_ap, waits):
        mean = scr[:, SB:SB+1]
        var = scr[:, SB+2:SB+3]
        rstd = scr[:, SB+1:SB+2]
        t1 = pb.op("vector", lambda e: e.reduce_sum(mean, src_ap, AX.X),
                   waits=L(*waits), inc=sV)
        t2 = pb.op("vector", lambda e: e.tensor_single_scalar(mean, mean, 1.0/D, ALU.mult),
                   waits=[t1], inc=sV)
        t3 = pb.op("vector", lambda e: e.tensor_scalar(ctile, src_ap, mean, None, ALU.subtract),
                   waits=L(t2, st.get("ctile_r"), *waits), inc=sV)
        t4 = pb.op("scalar", lambda e: e.activation(garb_ap, ctile, AF.Square, accum_out=var),
                   waits=L(t3, st.get("garb_w"), st.get("ftile_r"), st.get("tmp1_r")), inc=sA)
        st["ctile_r"] = t4
        st["garb_w"] = t4
        t5 = pb.op("scalar", lambda e: e.activation(var, var, AF.Ln, bias=eps_col, scale=1.0/D),
                   waits=L(t_consts), wforce=[t4], inc=sA)
        t6 = pb.op("scalar", lambda e: e.activation(rstd, var, AF.Exp, scale=-0.5),
                   waits=L(t5, st.get("rstd_r")), inc=sA)
        return t3, t6, t4

    def transpose_tile_to(dst_ap, src_tile_ap, nchunk, waits):
        tks = []
        for c in range(nchunk):
            tks.append(pb.op("tensor", lambda e, c=c: e.transpose(
                ptr[:, c*128:(c+1)*128], src_tile_ap[:, c*128:(c+1)*128], sident),
                waits=L(t_id, st.get("ptr_r"), bank_r[7], *waits) if c == 0 else [],
                inc=sT))
        st["ptr_w"] = tks[-1]
        tev = pb.op("vector", lambda e: e.tensor_copy(
            dst_ap, ptr[:, 0:nchunk*128].rearrange("p (c t) -> p c t", c=nchunk)),
            waits=[tks[-1]], inc=sV)
        st["ptr_r"] = tev
        return tks[-1], tev

    def emit_ffn(layer):
        barrier = snap()
        o = 0
        o_featsT, o = o, o + 8192
        o_featsM, o = o, o + 8192
        o_h1t, o = o, o + 32768
        o_w1, o = o, o + 2048
        o_w2, o = o, o + 4096
        o_wr, o = o, o + 64
        o_ftile, o = o, o + 1024
        o_maskb, o = o, o + 8
        o_mrow, o = o, o + 8192
        o_b2s, o = o, o + 512
        assert o <= ARENA_N, o
        featsT = arena[:, o_featsT:o_featsT+8192].rearrange("p (c t) -> p c t", c=8)
        featsM = arena[:, o_featsM:o_featsM+8192].rearrange("p (c t) -> p c t", c=8)
        h1t = arena[:, o_h1t:o_h1t+32768].rearrange("p (c t) -> p c t", c=32)
        w1b = arena[:, o_w1:o_w1+2048].rearrange("p (b c t) -> p b c t", b=2, c=8)
        w2b = arena[:, o_w2:o_w2+4096].rearrange("p (b c t) -> p b c t", b=2, c=4)
        wrsb = arena[:, o_wr:o_wr+64].rearrange("p (c e) -> p c e", c=8)
        ftile = arena[:, o_ftile:o_ftile+1024]
        tmp1 = arena[:, o_ftile:o_ftile+512]
        maskb = arena[:, o_maskb:o_maskb+8]
        mrow = arena[0:1, o_mrow:o_mrow+8192].rearrange("p (e t) -> p e t", e=8)
        b2slot = arena[0:1, o_b2s:o_b2s+512]

        c0 = PH
        o_route, c0 = c0, c0 + 8
        o_r2, c0 = c0, c0 + 8
        o_r3, c0 = c0, c0 + 8
        o_brb, c0 = c0, c0 + 8
        o_sblk, c0 = c0, c0 + 8
        o_ones8, c0 = c0, c0 + 8
        o_gam, c0 = c0, c0 + 64
        o_bet, c0 = c0, c0 + 64
        o_b1, c0 = c0, c0 + 256
        assert c0 <= 2560
        route = scr[:, o_route:o_route+8]
        r2 = scr[:, o_r2:o_r2+8]
        r3 = scr[:, o_r3:o_r3+8]
        brb = scr[:, o_brb:o_brb+8]
        sblk = scr[:, o_sblk:o_sblk+8]
        ones8 = scr[:, o_ones8:o_ones8+8]

        if layer == 0:
            w1d, w2d, wrd = dr["f0_w1"], dr["f0_w2"], dr["f0_wr"]
        else:
            w1d, w2d, wrd = dr["f1_w1t"], dr["f1_w2t"], dr["f1_cent_t"]

        t_o8 = pb.op("vector", lambda e: e.memset(ones8, 1.0), waits=barrier, inc=sV)

        wr3 = wrd.rearrange("(c p) e -> p c e", p=128)
        t_wr = pb.dma("gpsimd", wrsb[:, :, :], wr3[:, :, :],
                      waits=L(st.get("wrsb_r"), *barrier), inc=SL["wr"])
        if layer == 0:
            t_brrow = pb.dma("sync", scr[0:1, ROWB:ROWB+8], dr["f0_br"][None, :],
                             waits=barrier, inc=sC)
            tb = pb.op("tensor", lambda e: e.matmul(p32[:, 6, 0:8], sones_f32,
                                                    scr[0:1, ROWB:ROWB+8],
                                                    start=True, stop=True),
                       waits=L(t_of, t_brrow, bank_r[6], *barrier), inc=sT)
            bank_w[6] = tb
            t_brb = pb.op("vector", lambda e: e.tensor_copy(brb, p32[:, 6, 0:8]),
                          waits=[tb], inc=sV)
            bank_r[6] = t_brb
            t_gb = t_b1 = None
            gam = bet = b1sb = None
        else:
            gam = scr[:, o_gam:o_gam+64].rearrange("p (c e) -> p c e", c=8)
            bet = scr[:, o_bet:o_bet+64].rearrange("p (c e) -> p c e", c=8)
            b1sb = scr[:, o_b1:o_b1+256].rearrange("p (c e) -> p c e", c=32)
            g3 = dr["f1_gam_t"].rearrange("(c p) e -> p c e", p=128)
            b3 = dr["f1_bet_t"].rearrange("(c p) e -> p c e", p=128)
            b13 = dr["f1_b1t"].rearrange("(c p) e -> p c e", p=128)
            tg1 = pb.dma("sync", gam[:, :, :], g3[:, :, :], waits=barrier, inc=sC)
            t_gb = pb.dma("sync", bet[:, :, :], b3[:, :, :], inc=sC)
            t_b1 = pb.dma("sync", b1sb[:, :, :], b13[:, :, :], inc=sC)
            t_gb = t_b1 = (sC, sC.n)
            tk = None
            for c in range(8):
                tk = pb.op("tensor", lambda e, c=c: e.matmul(
                    p32[0:1, 6, 0:8], sonescol_bf, wrsb[:, c, :],
                    start=(c == 0), stop=(c == 7)),
                    waits=L(t_oc, t_wr, bank_r[6], *barrier) if c == 0 else [], inc=sT)
            bank_w[6] = tk
            t_csr = pb.op("vector", lambda e: e.tensor_copy(scr[0:1, ROWB+8:ROWB+16],
                                                            p32[0:1, 6, 0:8]),
                          waits=[tk], inc=sV)
            bank_r[6] = t_csr
            tb = pb.op("tensor", lambda e: e.matmul(p32[:, 6, 0:8], sones_f32,
                                                    scr[0:1, ROWB+8:ROWB+16],
                                                    start=True, stop=True),
                       waits=L(t_of, t_csr), inc=sT)
            bank_w[6] = tb
            t_brb = pb.op("vector", lambda e: e.tensor_copy(brb, p32[:, 6, 0:8]),
                          waits=[tb], inc=sV)
            bank_r[6] = t_brb

        tok_w1_mm_last = None
        tok_w2_mm_last = None
        tok_w1buf_cons = [None, None]
        tok_w2buf_cons = [None, None]
        tok_b2s_cons = None

        for blk in range(4):
            tok_route_done = []
            tok_mrow_w = None
            for j in range(8):
                tile = blk*8 + j
                xsrc = xres[:, tile, :]
                tc, trs, tsq = ln_tile(xsrc, ccol("ln_eps"), ftile,
                                       L(xw(tile), st.get("ftile_r"), *barrier))
                rstd1 = scr[:, SB+1:SB+2]
                tf = pb.op("vector", lambda e: e.tensor_scalar(
                    ftile, ctile, rstd1, None, ALU.mult),
                    waits=L(tc, trs, tsq, st.get("ftile_r")), inc=sV)
                st["rstd_r"] = tf
                st["garb_w"] = tf
                if layer == 1:
                    tra = pb.op("vector", lambda e, tile=tile: e.scalar_tensor_tensor(
                        xres[:, tile, :], ftile, 0.5, xres[:, tile, :], ALU.mult, ALU.add),
                        waits=L(tf, xw(tile), xr(tile)), inc=sV)
                    set_xw(tile, tra)
                    mu2 = scr[:, SB+4:SB+5]
                    var2 = scr[:, SB+5:SB+6]
                    rstd2 = scr[:, SB+6:SB+7]
                    invr2 = scr[:, SB+7:SB+8]
                    u1 = pb.op("vector", lambda e: e.reduce_sum(mu2, ftile, AX.X),
                               waits=L(tf, st.get("mu2_r")), inc=sV)
                    u2 = pb.op("vector", lambda e: e.tensor_single_scalar(
                        mu2, mu2, 1.0/D, ALU.mult), waits=[u1], inc=sV)
                    u3 = pb.op("vector", lambda e: e.tensor_scalar(
                        ctile, ftile, mu2, None, ALU.subtract),
                        waits=L(u2, st.get("ctile_r")), inc=sV)
                    u4 = pb.op("scalar", lambda e: e.activation(
                        ftile, ctile, AF.Square, accum_out=var2), waits=[u3], inc=sA)
                    st["ctile_r"] = u4
                    u5 = pb.op("scalar", lambda e: e.activation(
                        var2, var2, AF.Ln, bias=ccol("ln_eps"), scale=1.0/D),
                        waits=[], wforce=[u4], inc=sA)
                    u6 = pb.op("scalar", lambda e: e.activation(
                        rstd2, var2, AF.Exp, scale=-0.5),
                        waits=L(u5, st.get("rstd2_r")), inc=sA)
                    u7 = pb.op("scalar", lambda e: e.activation(
                        invr2, var2, AF.Exp, scale=0.5),
                        waits=L(u5, st.get("invr2_r")), inc=sA)
                    tf2 = pb.op("vector", lambda e: e.tensor_scalar(
                        ftile, ctile, rstd2, None, ALU.mult), waits=L(u3, u6, u4), inc=sV)
                    st["rstd2_r"] = tf2
                    tfin = tf2
                else:
                    u2 = u7 = None
                    tfin = tf
                ttr, tev = transpose_tile_to(featsT[:, :, j*128:(j+1)*128], ftile, 8,
                                             [tfin, tok_w1_mm_last])
                st["ftile_r"] = ttr
                tk = None
                for c in range(8):
                    tk = pb.op("tensor", lambda e, c=c, j=j: e.matmul(
                        p32[:, 6, 0:8], featsT[:, c, j*128:(j+1)*128], wrsb[:, c, :],
                        start=(c == 0), stop=(c == 7)),
                        waits=L(tev, t_wr, bank_r[6]) if c == 0 else [], inc=sT)
                bank_w[6] = tk
                if layer == 0:
                    tl = pb.op("vector", lambda e: e.tensor_tensor(
                        route, p32[:, 6, 0:8], brb, ALU.add),
                        waits=L(tk, t_brb), inc=sV)
                else:
                    ta1 = pb.op("vector", lambda e: e.tensor_scalar(
                        route, p32[:, 6, 0:8], scr[:, SB+7:SB+8], None, ALU.mult),
                        waits=L(tk, u7), inc=sV)
                    st["invr2_r"] = ta1
                    ta2 = pb.op("vector", lambda e: e.tensor_scalar(
                        r2, brb, scr[:, SB+4:SB+5], None, ALU.mult),
                        waits=L(t_brb, u2), inc=sV)
                    st["mu2_r"] = ta2
                    tl = pb.op("vector", lambda e: e.tensor_tensor(
                        route, route, r2, ALU.add), waits=[ta1, ta2], inc=sV)
                bank_r[6] = tl
                maxl = scr[:, SB+8:SB+9]
                eneg = scr[:, SB+9:SB+10]
                tm = pb.op("vector", lambda e: e.reduce_max(maxl, route, AX.X),
                           waits=L(tl, st.get("maxl_r")), inc=sV)
                ts1 = pb.op("scalar", lambda e: e.activation(eneg, maxl, AF.Exp, scale=-1.0),
                            waits=L(tm, st.get("eneg_r")), inc=sA)
                st["maxl_r"] = ts1
                ts2 = pb.op("vector", lambda e: e.tensor_single_scalar(
                    eneg, eneg, 1.0, ALU.add), waits=[ts1], inc=sV)
                ts3 = pb.op("vector", lambda e: e.reciprocal(eneg, eneg), waits=[ts2], inc=sV)
                ts4 = pb.op("vector", lambda e, j=j: e.tensor_single_scalar(
                    sblk[:, j:j+1], eneg, 0.5, ALU.mult), waits=[ts3], inc=sV)
                st["eneg_r"] = ts4
                tok_route_done.append(ts4)
                tq1 = pb.op("vector", lambda e: e.tensor_scalar(
                    r2, route, maxl, None, ALU.is_equal), waits=[tm], inc=sV)
                tq2 = pb.op("vector", lambda e: e.tensor_tensor_scan(
                    r3, ones8, r2, 0.0, ALU.mult, ALU.add),
                    waits=L(tq1, t_o8), inc=sV)
                tq3 = pb.op("vector", lambda e: e.tensor_single_scalar(
                    r3, r3, 1.0, ALU.is_equal), waits=[tq2], inc=sV)
                tq4 = pb.op("vector", lambda e: e.tensor_tensor(
                    r2, r2, r3, ALU.mult), waits=[tq3], inc=sV)
                tq5 = pb.op("vector", lambda e: e.tensor_copy(maskb, r2),
                            waits=L(tq4, st.get("maskb_r")), inc=sV)
                tks = []
                for ee in range(NE):
                    tks.append(pb.op("tensor", lambda e, ee=ee: e.transpose(
                        ptr[0:1, ee*128:(ee+1)*128], maskb[:, ee:ee+1], sident),
                        waits=L(tq5, st.get("ptr_r"), t_id) if ee == 0 else [], inc=sT))
                st["ptr_w"] = tks[-1]
                st["maskb_r"] = tks[-1]
                tq6 = pb.op("vector", lambda e, j=j: e.tensor_copy(
                    mrow[0:1, :, j*128:(j+1)*128],
                    ptr[0:1, 0:1024].rearrange("p (e t) -> p e t", e=8)),
                    waits=[tks[-1]], inc=sV)
                st["ptr_r"] = tq6
                tok_mrow_w = tq6

            for e_i in range(NE):
                tpk = None
                for g in range(2):
                    tpk = pb.op("tensor", lambda e, e_i=e_i, g=g: e.matmul(
                        p32[:, g, :], sones_bf, mrow[0:1, e_i, g*512:(g+1)*512],
                        start=True, stop=True),
                        waits=L(t_ob, tok_mrow_w, bank_r[0], bank_r[1]) if g == 0
                        else [], inc=sT)
                bank_w[0] = bank_w[1] = tpk
                if layer == 0:
                    tmask = None
                    for c in range(8):
                        for g in range(2):
                            tmask = pb.op("vector", lambda e, c=c, g=g: e.tensor_tensor(
                                featsM[:, c, g*512:(g+1)*512],
                                featsT[:, c, g*512:(g+1)*512], p32[:, g, :], ALU.mult),
                                waits=L(tpk, tok_w1_mm_last) if (c == 0 and g == 0)
                                else [], inc=sV)
                    bank_r[0] = bank_r[1] = tmask
                else:
                    tmask = None
                    for c in range(8):
                        tmask = pb.op("vector", lambda e, c=c, e_i=e_i: e.tensor_scalar(
                            featsM[:, c, :], featsT[:, c, :],
                            gam[:, c, e_i:e_i+1], bet[:, c, e_i:e_i+1],
                            ALU.mult, ALU.add),
                            waits=L(t_gb, tok_w1_mm_last) if c == 0 else [], inc=sV)
                tok_featsM_w = tmask

                tok_h1_ev = None
                for n in range(32):
                    par = n % 2
                    w1src = w1d[e_i, n].rearrange("p (c t) -> p c t", c=8)
                    tld = pb.dma("gpsimd", w1b[:, par, :, :], w1src,
                                 waits=L(tok_w1buf_cons[par], *barrier),
                                 inc=SL["w1a" if par == 0 else "w1b"])
                    for g in range(2):
                        bk = 4 + ((n*2 + g) % 2)
                        tk = None
                        for dd in range(8):
                            tk = pb.op("tensor",
                                lambda e, dd=dd, g=g, bk=bk, par=par: e.matmul(
                                    p32[:, bk, :],
                                    w1b[:, par, dd, :],
                                    featsM[:, dd, g*512:(g+1)*512],
                                    start=(dd == 0), stop=(dd == 7)),
                                waits=L(tld, tok_featsM_w, bank_r[bk])
                                if dd == 0 else [], inc=sT)
                        bank_w[bk] = tk
                        if g == 1:
                            tok_w1buf_cons[par] = tk
                        slot = h1t[:, n, g*512:(g+1)*512]
                        if layer == 0:
                            tev = pb.op("scalar",
                                lambda e, slot=slot, bk=bk: e.activation(
                                    slot, p32[:, bk, :], AF.Gelu),
                                waits=L(tk, tok_w2_mm_last), inc=sA)
                            bank_r[bk] = tev
                        else:
                            trl = pb.op("scalar",
                                lambda e, bk=bk, n=n, e_i=e_i: e.activation(
                                    tmp1, p32[:, bk, 0:512], AF.Relu,
                                    bias=b1sb[:, n, e_i:e_i+1]),
                                waits=L(tk, t_b1, st.get("tmp1_r"), st.get("ftile_r"),
                                        st.get("garb_w")), inc=sA)
                            bank_r[bk] = trl
                            st["garb_w"] = trl
                            tev = pb.op("vector",
                                lambda e, slot=slot, g=g: e.tensor_tensor(
                                    slot, tmp1, p32[:, g, :], ALU.mult),
                                waits=L(trl, tpk, tok_w2_mm_last), inc=sV)
                            st["tmp1_r"] = tev
                            bank_r[0] = bank_r[1] = tev
                        tok_h1_ev = tev
                tok_w1_mm_last = tok_w1buf_cons[0] \
                    if tok_w1buf_cons[0][1] > tok_w1buf_cons[1][1] else tok_w1buf_cons[1]
                st["ftile_r"] = st.get("tmp1_r") if layer == 1 else st.get("ftile_r")

                for hd in range(2):
                    if layer == 1:
                        tb2 = pb.dma("gpsimd", b2slot,
                                     dr["f1_b2"][e_i, hd*512:(hd+1)*512][None, :],
                                     waits=L(tok_b2s_cons, *barrier), inc=SL["b2"])
                    last_mm = [None]*8
                    for n in range(8):
                        par = n % 2
                        w2src = w2d[e_i, hd, n].rearrange("p (c t) -> p c t", c=4)
                        tld = pb.dma("gpsimd", w2b[:, par, :, :], w2src,
                                     waits=L(tok_w2buf_cons[par], *barrier),
                                     inc=SL["w2a" if par == 0 else "w2b"])
                        for fm in range(4):
                            for s_ in range(8):
                                stop_ = (layer == 0 and n == 7 and fm == 3)
                                tk = pb.op("tensor",
                                    lambda e, n=n, fm=fm, s_=s_, par=par, stop_=stop_:
                                    e.matmul(
                                        pbank(s_),
                                        h1t[:, n*4+fm, s_*128:(s_+1)*128],
                                        w2b[:, par, fm, :],
                                        start=(n == 0 and fm == 0), stop=stop_),
                                    waits=L(tld, tok_h1_ev,
                                            bank_r[s_] if (n == 0 and fm == 0) else None)
                                    if (fm == 0) else [], inc=sT)
                                last_mm[s_] = tk
                            if fm == 3:
                                tok_w2buf_cons[par] = tk
                    if layer == 1:
                        for s_ in range(8):
                            tk = pb.op("tensor",
                                lambda e, s_=s_, e_i=e_i: e.matmul(
                                    pbank(s_),
                                    mrow[0:1, e_i, s_*128:(s_+1)*128],
                                    b2slot,
                                    start=False, stop=True),
                                waits=L(tb2, tok_mrow_w) if s_ == 0 else [], inc=sT)
                            last_mm[s_] = tk
                        tok_b2s_cons = tk
                    for s_ in range(8):
                        bank_w[min(s_, 6)] = last_mm[s_]
                    for s_ in range(8):
                        tile = blk*8 + s_
                        tev = pb.op("vector",
                            lambda e, tile=tile, s_=s_, hd=hd:
                            e.scalar_tensor_tensor(
                                xres[:, tile, hd*512:(hd+1)*512],
                                pbank(s_), sblk[:, s_:s_+1],
                                xres[:, tile, hd*512:(hd+1)*512],
                                ALU.mult, ALU.add),
                            waits=L(last_mm[s_], tok_route_done[s_], xw(tile), xr(tile)),
                            inc=sV)
                        bank_r[s_] = tev
                        set_xw(tile, tev)
                    tok_w2_mm_last = last_mm[7]
        st["wrsb_r"] = tok_w1_mm_last

    def emit_attn0():
        barrier = snap()
        o = 0
        o_hT, o = o, o + 4096
        o_qp, o = o, o + 4160
        o_kp, o = o, o + 4160
        o_vp, o = o, o + 8320
        o_gt, o = o, o + 8192
        o_kt, o = o, o + 1024
        o_vt, o = o, o + 2048
        o_pf, o = o, o + 8192
        o_mats, o = o, o + 2048
        o_bd, o = o, o + 1024
        o_S, o = o, o + 4096
        o_wp, o = o, o + 2048
        o_wo, o = o, o + 4096
        o_lnv, o = o, o + 6144
        o_ft, o = o, o + 1024
        o_xb, o = o, o + 4096
        assert o <= ARENA_N, o
        hT = arena[:, o_hT:o_hT+4096].rearrange("p (c t) -> p c t", c=8)
        qp = arena[:, o_qp:o_qp+4160].rearrange("p (c t) -> p c t", c=8)
        kp = arena[:, o_kp:o_kp+4160].rearrange("p (c t) -> p c t", c=8)
        vp = arena[:, o_vp:o_vp+8320].rearrange("p (c t) -> p c t", c=16)
        gt = arena[:, o_gt:o_gt+8192].rearrange("p (c t) -> p c t", c=16)
        kt = arena[:, o_kt:o_kt+1024]
        vtk = arena[:, o_vt:o_vt+2048]
        pf = arena[:, o_pf:o_pf+8192].rearrange("p (c t) -> p c t", c=16)
        mats = arena[:, o_mats:o_mats+2048].rearrange("p (c t) -> p c t", c=16)
        bd = arena[:, o_bd:o_bd+1024].rearrange("p (c t) -> p c t", c=2)
        Sst = arena[:, o_S:o_S+4096].rearrange("p (c t) -> p c t", c=8)
        wp = arena[:, o_wp:o_wp+2048].rearrange("p (b c t) -> p b c t", b=2, c=8)
        wo_ = arena[:, o_wo:o_wo+4096].rearrange("p (b c t) -> p b c t", b=2, c=16)
        lnv = arena[:, o_lnv:o_lnv+6144].rearrange("p (c t) -> p c t", c=6)
        ftile = arena[:, o_ft:o_ft+1024]
        xblk = arena[:, o_xb:o_xb+4096].rearrange("p (c t) -> p c t", c=4)
        garb = ftile

        M_SK0, M_SK1 = 0, 1
        M_N, M_NT = 2, 3
        M_P, M_PT = 4, 5
        M_XB, M_XBT = 6, 7
        M_DS = 8
        M_ET = 9
        M_DST = 10
        M_GT = 12
        gtokv = arena[:, o_mats+M_GT*128:o_mats+(M_GT+4)*128]

        c0 = PH
        o_cw, c0 = c0, c0 + 128
        o_onw, c0 = c0, c0 + 4
        c0 += 2
        o_gsc, c0 = c0, c0 + 512
        o_bsc, c0 = c0, c0 + 512
        o_gca, c0 = c0, c0 + 1
        o_gcol, c0 = c0, c0 + 4
        o_lam, c0 = c0, c0 + 4
        o_bco, c0 = c0, c0 + 4
        o_nbc, c0 = c0, c0 + 4
        o_khs, c0 = c0, c0 + 1
        o_lcc, c0 = c0, c0 + 1
        o_gcc, c0 = c0, c0 + 1
        o_tc1, c0 = c0, c0 + 1
        o_ro, c0 = c0, c0 + 1
        c0 += 16
        o_nal, c0 = c0, c0 + 1
        o_dtb, c0 = c0, c0 + 1
        assert c0 <= 2560, c0
        cw = scr[:, o_cw:o_cw+128].rearrange("p (c t) -> p c t", c=32)
        onw = scr[:, o_onw:o_onw+4]
        nal = scr[:, o_nal:o_nal+1]
        dtb = scr[:, o_dtb:o_dtb+1]
        gsc = scr[0:4, o_gsc:o_gsc+512]
        bsc = scr[0:4, o_bsc:o_bsc+512]
        gca = scr[0:4, o_gca:o_gca+1]
        gcol = scr[:, o_gcol:o_gcol+4]
        lam = scr[:, o_lam:o_lam+4]
        bco = scr[:, o_bco:o_bco+4]
        nbc = scr[:, o_nbc:o_nbc+4]
        khs = scr[:, o_khs:o_khs+1]
        lcc = scr[:, o_lcc:o_lcc+1]
        gcc = scr[:, o_gcc:o_gcc+1]
        tc1 = scr[:, o_tc1:o_tc1+1]
        rocol = scr[:, o_ro:o_ro+1]
        convacc = scr[:, 0:512]

        tcw = None
        for nm, off, nch in (("a0_cq", 0, 8), ("a0_ck", 8, 8), ("a0_cv", 16, 16)):
            src = dr[nm].rearrange("(c p) k -> p c k", p=128)
            tcw = pb.dma("sync", cw[:, off:off+nch, :], src, waits=L(tcw, *barrier),
                         inc=sC, slow=True)
        t_onw = pb.dma("sync", onw, dr["a0_onw"].rearrange("(c p) -> p c", p=128),
                       waits=[tcw], inc=sC, slow=True)
        t_al = pb.dma("sync", scr[0:4, o_nal:o_nal+1],
                      dr["a0_alog"].rearrange("(h o) -> h o", o=1),
                      waits=[t_onw], inc=sC)
        t_dtb = pb.dma("sync", scr[0:4, o_dtb:o_dtb+1],
                       dr["a0_dtb"].rearrange("(h o) -> h o", o=1),
                       waits=[t_al], inc=sC)
        t_cfg = (sC, sC.n)
        tna = pb.op("scalar", lambda e: e.activation(
            scr[0:4, o_nal:o_nal+1], scr[0:4, o_nal:o_nal+1], AF.Exp),
            waits=L(t_cfg, *barrier), inc=sA)
        tna = pb.op("vector", lambda e: e.tensor_single_scalar(
            scr[0:4, o_nal:o_nal+1], scr[0:4, o_nal:o_nal+1], -1.0, ALU.mult),
            waits=[tna], inc=sV)
        t_lb = None
        for li, nm in enumerate(("a0_ln1_g", "a0_ln1_b", "a0_ln2_g", "a0_ln2_b",
                                 "a0_rms_w", "a0_b2")):
            for hh in range(2):
                trow = pb.dma("sync", scr[0:1, 512:1024], dr[nm][None, hh*512:(hh+1)*512],
                              waits=L(t_lb, st.get("ctile_r"), *barrier), inc=sC)
                tbm = pb.op("tensor", lambda e: e.matmul(
                    p32[:, 6, :], sones_f32, scr[0:1, 512:1024], start=True, stop=True),
                    waits=L(t_of, trow, bank_r[6], *barrier), inc=sT)
                bank_w[6] = tbm
                t_lb = pb.op("vector", lambda e, li=li, hh=hh: e.tensor_copy(
                    lnv[:, li, hh*512:(hh+1)*512], p32[:, 6, :]),
                    waits=L(tbm, st.get("lnv_r")), inc=sV)
                bank_r[6] = t_lb
        t_lnv = t_lb

        smallbk = [4, 5]
        sb_i = [0]
        def nextbank():
            b = smallbk[sb_i[0] % 2]
            sb_i[0] += 1
            return b

        for sq in range(SEQ_PER_CORE):
            base = sq * 16
            tS0 = pb.op("vector", lambda e: e.memset(Sst[:, :, :], 0.0),
                        waits=L(st.get("S_r"), *barrier), inc=sV)
            th0 = pb.op("vector", lambda e: e.memset(qp[:, :, 0:3], 0.0), waits=barrier, inc=sV)
            th1 = pb.op("vector", lambda e: e.memset(kp[:, :, 0:3], 0.0), waits=[th0], inc=sV)
            th2 = pb.op("vector", lambda e: e.memset(vp[:, :, 0:3], 0.0), waits=[th1], inc=sV)
            t_halo = th2

            for sc in range(4):
                for j4 in range(4):
                    tile = base + sc*4 + j4
                    tc, trs, tsq = ln_tile(xres[:, tile, :], ccol("ln_eps"), garb,
                                           L(xw(tile), st.get("ftile_r"), *barrier))
                    set_xr(tile, tc)
                    tx = pb.op("vector", lambda e: e.tensor_scalar(
                        ctile, ctile, scr[:, SB+1:SB+2], None, ALU.mult),
                        waits=L(tc, trs, st.get("ctile_r")), inc=sV)
                    st["rstd_r"] = tx
                    txb = pb.op("vector", lambda e, j4=j4: e.tensor_copy(
                        xblk[:, j4, :], ctile), waits=L(tx, st.get("xblk_r")), inc=sV)
                    mu2 = scr[:, SB+4:SB+5]
                    var2 = scr[:, SB+5:SB+6]
                    rstd2 = scr[:, SB+6:SB+7]
                    u1 = pb.op("vector", lambda e: e.reduce_sum(mu2, ctile, AX.X),
                               waits=[tx], inc=sV)
                    u2 = pb.op("vector", lambda e: e.tensor_single_scalar(
                        mu2, mu2, 1.0/D, ALU.mult), waits=[u1], inc=sV)
                    u3 = pb.op("vector", lambda e: e.tensor_scalar(
                        ctile, ctile, mu2, None, ALU.subtract), waits=[u2], inc=sV)
                    u4 = pb.op("scalar", lambda e: e.activation(
                        garb, ctile, AF.Square, accum_out=var2),
                        waits=L(u3, st.get("ftile_r"), st.get("garb_w")), inc=sA)
                    st["garb_w"] = u4
                    u5 = pb.op("scalar", lambda e: e.activation(
                        var2, var2, AF.Ln, bias=ccol("ln_eps"), scale=1.0/D),
                        waits=L(u4, t_consts), inc=sA)
                    u6 = pb.op("scalar", lambda e: e.activation(
                        rstd2, var2, AF.Exp, scale=-0.5),
                        waits=L(u5, st.get("rstd2_r")), inc=sA)
                    u8 = pb.op("vector", lambda e: e.tensor_scalar(
                        ctile, ctile, rstd2, None, ALU.mult), waits=L(u3, u6, u4), inc=sV)
                    st["rstd2_r"] = u8
                    u9 = pb.op("vector", lambda e: e.tensor_tensor(
                        ctile, ctile, lnv[:, 0, :], ALU.mult), waits=L(u8, t_lnv), inc=sV)
                    u10 = pb.op("vector", lambda e: e.tensor_tensor(
                        ctile, ctile, lnv[:, 1, :], ALU.add), waits=[u9], inc=sV)
                    var3 = scr[:, SB+5:SB+6]
                    v1 = pb.op("scalar", lambda e: e.activation(
                        garb, ctile, AF.Square, accum_out=var3),
                        waits=L(u10, st.get("garb_w")), inc=sA)
                    st["garb_w"] = v1
                    v2 = pb.op("scalar", lambda e: e.activation(
                        var3, var3, AF.Ln, bias=ccol("rms_eps"), scale=1.0/D),
                        waits=[], wforce=[v1], inc=sA)
                    v3 = pb.op("scalar", lambda e: e.activation(
                        scr[:, SB+6:SB+7], var3, AF.Exp, scale=-0.5),
                        waits=L(v2, st.get("rstd2_r")), inc=sA)
                    st["rstd2_r"] = v3
                    v4 = pb.op("vector", lambda e: e.scalar_tensor_tensor(
                        ftile, ctile, scr[:, SB+6:SB+7], lnv[:, 4, :], ALU.mult, ALU.mult),
                        waits=L(v3, v1, st.get("ftile_r"), st.get("garb_w")), inc=sV)
                    st["ctile_r"] = v4
                    st["garb_w"] = v4
                    ttr, tev = transpose_tile_to(hT[:, :, j4*128:(j4+1)*128], ftile, 8,
                                                 [v4, st.get("hT_r")])
                    st["ftile_r"] = ttr
                t_hT = tev

                projs = (("a0_wq", qp, 8, "q"), ("a0_wk", kp, 8, "k"),
                         ("a0_wv", vp, 16, "v"), ("a0_wg", gt, 16, "g"))
                wpar = [0]
                tok_wp_cons = [st.get("wp0_r"), st.get("wp1_r")]
                for nm, dst, nch, kind in projs:
                    w3 = dr[nm]
                    for oc in range(nch):
                        par = wpar[0] % 2
                        wpar[0] += 1
                        tld = pb.dma("gpsimd", wp[:, par, :, :],
                                     w3[oc].rearrange("p (c t) -> p c t", c=8),
                                     waits=L(tok_wp_cons[par], *barrier),
                                     inc=SL["wc" if par == 0 else "wd"])
                        bkp = nextbank()
                        tk = None
                        for dd in range(8):
                            tk = pb.op("tensor", lambda e, dd=dd, par=par, bkp=bkp: e.matmul(
                                p32[:, bkp, :], wp[:, par, dd, :], hT[:, dd, :],
                                start=(dd == 0), stop=(dd == 7)),
                                waits=L(tld, t_hT, bank_r[bkp]) if dd == 0 else [], inc=sT)
                        bank_w[bkp] = tk
                        tok_wp_cons[par] = tk
                        if kind == "g":
                            te1 = pb.op("scalar", lambda e, oc=oc, bkp=bkp: e.activation(
                                gt[:, oc, :], p32[:, bkp, :], AF.Silu),
                                waits=L(tk, st.get("gt_r")), inc=sA)
                            te2 = pb.op("vector", lambda e, oc=oc: e.tensor_scalar(
                                gt[:, oc, :], gt[:, oc, :], onw[:, (oc % 4):(oc % 4)+1],
                                None, ALU.mult), waits=L(te1, t_cfg), inc=sV)
                            bank_r[bkp] = te2
                        else:
                            te1 = pb.op("vector", lambda e, oc=oc, dst=dst, bkp=bkp:
                                e.tensor_copy(dst[:, oc, 3:515], p32[:, bkp, :]),
                                waits=L(tk, t_halo, st.get(kind + "_pre_r")), inc=sV)
                            bank_r[bkp] = te1
                t_proj = (sV, sV.n)
                tok_ab = []
                for wi, (nm, dstc) in enumerate((("a0_wa", o_gsc), ("a0_wb", o_bsc))):
                    par = wpar[0] % 2
                    wpar[0] += 1
                    tld = pb.dma("gpsimd", wp[:, par, 0:8, 0:4],
                                 dr["a0_wat"][:, wi*32:(wi+1)*32].rearrange(
                                     "p (c t) -> p c t", c=8),
                                 waits=L(tok_wp_cons[par], *barrier),
                                 inc=SL["wc" if par == 0 else "wd"])
                    tk = None
                    for dd in range(8):
                        tk = pb.op("tensor", lambda e, dd=dd, par=par: e.matmul(
                            p32[0:4, 6, :], wp[:, par, dd, 0:4], hT[:, dd, :],
                            start=(dd == 0), stop=(dd == 7)),
                            waits=L(tld, t_hT, bank_r[6]) if dd == 0 else [], inc=sT)
                    bank_w[6] = tk
                    tok_wp_cons[par] = tk
                    tev = pb.op("vector", lambda e, dstc=dstc: e.tensor_copy(
                        scr[0:4, dstc:dstc+512], p32[0:4, 6, :]),
                        waits=L(tk, st.get("gsc_r")), inc=sV)
                    bank_r[6] = tev
                    tok_ab.append(tev)
                import os as _os
                if _os.environ.get("DBG_STOP") == "evac":
                    return
                ga1 = pb.op("vector", lambda e: e.tensor_scalar(
                    gsc, gsc, scr[0:4, o_dtb:o_dtb+1], None, ALU.add),
                    waits=L(tok_ab[0], t_cfg), inc=sV)
                ga2 = pb.op("scalar", lambda e: e.activation(
                    scr[0:4, 512:1024], gsc, AF.Abs), waits=L(ga1, st.get("ctile_r")), inc=sA)
                ga3 = pb.op("scalar", lambda e: e.activation(
                    scr[0:4, 512:1024], scr[0:4, 512:1024], AF.Exp, scale=-1.0),
                    waits=[ga2], inc=sA)
                ga4 = pb.op("scalar", lambda e: e.activation(
                    scr[0:4, 512:1024], scr[0:4, 512:1024], AF.Ln, bias=1.0),
                    waits=[ga3], inc=sA)
                ga5 = pb.op("vector", lambda e: e.tensor_scalar_max(
                    gsc, gsc, 0.0), waits=L(ga1, ga2), inc=sV)
                ga6 = pb.op("vector", lambda e: e.tensor_tensor(
                    gsc, gsc, scr[0:4, 512:1024], ALU.add), waits=[ga5, ga4], inc=sV)
                st["ctile_r"] = ga6
                ga7 = pb.op("vector", lambda e: e.tensor_scalar(
                    gsc, gsc, scr[0:4, o_nal:o_nal+1], None, ALU.mult),
                    waits=L(ga6, tna), inc=sV)
                tmp4 = scr[0:4, 512:1024]
                A3 = gsc.rearrange("p (c t) -> p c t", c=4)
                B3 = tmp4.rearrange("p (c t) -> p c t", c=4)
                cur, oth = A3, B3
                tk8 = ga7
                for s_ in (1, 2, 4, 8, 16, 32, 64):
                    t1_ = pb.op("vector", lambda e, cur=cur, oth=oth, s_=s_: e.tensor_copy(
                        oth[:, :, 0:s_], cur[:, :, 0:s_]), waits=[tk8], inc=sV)
                    tk8 = pb.op("vector", lambda e, cur=cur, oth=oth, s_=s_: e.tensor_tensor(
                        oth[:, :, s_:128], cur[:, :, s_:128], cur[:, :, 0:128-s_],
                        ALU.add), waits=[t1_], inc=sV)
                    cur, oth = oth, cur
                ga8 = pb.op("vector", lambda e: e.tensor_copy(gsc, tmp4),
                            waits=[tk8], inc=sV)
                gb1 = pb.op("scalar", lambda e: e.activation(
                    bsc, bsc, AF.Exp, scale=-1.0), waits=[tok_ab[1]], inc=sA)
                gb2 = pb.op("vector", lambda e: e.tensor_single_scalar(
                    bsc, bsc, 1.0, ALU.add), waits=[gb1], inc=sV)
                gb3 = pb.op("vector", lambda e: e.reciprocal(bsc, bsc), waits=[gb2], inc=sV)
                import os as _os
                if _os.environ.get("DBG_STOP") == "chain":
                    return

                t_tail = None
                for ci, (buf, nch) in enumerate(((qp, 8), (kp, 8), (vp, 16))):
                    cwo = (0, 8, 16)[ci]
                    for c in range(nch):
                        a1 = pb.op("vector", lambda e, buf=buf, c=c, cwo=cwo: e.tensor_scalar(
                            convacc, buf[:, c, 0:512], cw[:, cwo+c, 0:1], None, ALU.mult),
                            waits=L(t_proj, tcw, st.get("cacc_r"), st.get("ctile_r"), ga8),
                            inc=sV)
                        for i in range(1, 4):
                            a1 = pb.op("vector", lambda e, buf=buf, c=c, cwo=cwo, i=i:
                                e.scalar_tensor_tensor(
                                    convacc, buf[:, c, i:i+512], cw[:, cwo+c, i:i+1],
                                    convacc, ALU.mult, ALU.add), waits=[a1], inc=sV)
                        a2 = pb.op("vector", lambda e, buf=buf, c=c: e.tensor_copy(
                            buf[:, c, 0:3], buf[:, c, 512:515]), waits=[a1], inc=sV)
                        a3 = pb.op("scalar", lambda e, buf=buf, c=c: e.activation(
                            buf[:, c, 3:515], convacc, AF.Silu), waits=L(a1, a2), inc=sA)
                        st["cacc_r"] = a3
                        t_tail = a3
                t_conv = t_tail
                st["ctile_r"] = t_conv

                for which, buf, scale in (("q", qp, 1.0/16), ("k", kp, 1.0)):
                    for h in range(4):
                        tsq2 = None
                        for cc in range(2):
                            tq = pb.op("scalar", lambda e, buf=buf, h=h, cc=cc: e.activation(
                                garb[:, 0:512], buf[:, 2*h+cc, 3:515], AF.Square),
                                waits=L(t_conv, st.get("garb_w"), st.get("ftile_r"), tsq2),
                                inc=sA)
                            st["garb_w"] = tq
                            tsq2 = pb.op("tensor", lambda e, cc=cc: e.matmul(
                                p32[0:1, 6, :], sonescol_bf, garb[:, 0:512],
                                start=(cc == 0), stop=(cc == 1)),
                                waits=L(tq, t_oc, bank_r[6]), inc=sT)
                        bank_w[6] = tsq2
                        tr1 = pb.op("vector", lambda e: e.tensor_copy(
                            scr[0:1, 512:1024], p32[0:1, 6, :]), waits=[tsq2], inc=sV)
                        bank_r[6] = tr1
                        tr2 = pb.op("scalar", lambda e: e.activation(
                            scr[0:1, 512:1024], scr[0:1, 512:1024], AF.Ln,
                            bias=ccol("rms_eps")[0:1, 0:1]), waits=L(tr1, t_consts), inc=sA)
                        tr3 = pb.op("scalar", lambda e: e.activation(
                            scr[0:1, 512:1024], scr[0:1, 512:1024], AF.Exp, scale=-0.5),
                            waits=[tr2], inc=sA)
                        if scale != 1.0:
                            tr3 = pb.op("vector", lambda e, scale=scale: e.tensor_single_scalar(
                                scr[0:1, 512:1024], scr[0:1, 512:1024], scale, ALU.mult),
                                waits=[tr3], inc=sV)
                        bkp = nextbank()
                        tb = pb.op("tensor", lambda e, bkp=bkp: e.matmul(
                            p32[:, bkp, :], sones_f32, scr[0:1, 512:1024],
                            start=True, stop=True),
                            waits=L(tr3, t_of, bank_r[bkp]), inc=sT)
                        bank_w[bkp] = tb
                        tm = None
                        for cc in range(2):
                            tm = pb.op("vector", lambda e, buf=buf, h=h, cc=cc, bkp=bkp:
                                e.tensor_tensor(
                                    buf[:, 2*h+cc, 3:515], buf[:, 2*h+cc, 3:515],
                                    p32[:, bkp, :], ALU.mult),
                                waits=[tb], inc=sV)
                        bank_r[bkp] = tm
                        st["cacc_r"] = tm
                t_norm = (sV, sV.n)

                for ch in range(4):
                    tks = []
                    for h in range(4):
                        for cc in range(2):
                            tks.append(pb.op("tensor", lambda e, h=h, cc=cc, ch=ch:
                                e.transpose(ptr[:, (h*2+cc)*128:(h*2+cc+1)*128],
                                            kp[:, 2*h+cc, 3+ch*128:3+(ch+1)*128], sident),
                                waits=L(t_norm, st.get("ptr_r"), bank_r[7], t_id)
                                if (h == 0 and cc == 0) else [], inc=sT))
                    st["ptr_w"] = tks[-1]
                    tke = pb.op("vector", lambda e, ch=ch: e.tensor_copy(
                        kt, ptr[:, 0:1024]),
                        waits=L(tks[-1], st.get("kt_r")), inc=sV)
                    st["ptr_r"] = tke
                    for half in range(2):
                        tks = []
                        for hv in range(8):
                            c = half*8 + hv
                            tks.append(pb.op("tensor", lambda e, c=c, ch=ch: e.transpose(
                                ptr[:, (c % 8)*128:((c % 8)+1)*128],
                                vp[:, c, 3+ch*128:3+(ch+1)*128], sident),
                                waits=L(st.get("ptr_r"), bank_r[7]) if hv == 0 else [],
                                inc=sT))
                        st["ptr_w"] = tks[-1]
                        tve = pb.op("vector", lambda e, ch=ch, half=half: e.tensor_copy(
                            vtk[:, half*1024:(half+1)*1024], ptr[:, 0:1024]),
                            waits=L(tks[-1], st.get("vt_r")), inc=sV)
                        st["ptr_r"] = tve
                    t_tok = tve
                    tg1 = pb.op("tensor", lambda e, ch=ch: e.transpose(
                        p32[:, 6, 0:4], gsc[0:4, ch*128:(ch+1)*128], sident32[0:4, 0:4]),
                        waits=L(ga8, bank_r[6], t_cm), inc=sT)
                    bank_w[6] = tg1
                    tg2 = pb.op("vector", lambda e: e.tensor_copy(gcol, p32[:, 6, 0:4]),
                                waits=L(tg1, st.get("gcol_r")), inc=sV)
                    bank_r[6] = tg2
                    tg3 = pb.op("tensor", lambda e, ch=ch: e.transpose(
                        p32[:, 6, 0:4], bsc[0:4, ch*128:(ch+1)*128], sident32[0:4, 0:4]),
                        waits=L(gb3, tg2, bank_r[6]), inc=sT)
                    bank_w[6] = tg3
                    tg4 = pb.op("vector", lambda e: e.tensor_copy(bco, p32[:, 6, 0:4]),
                                waits=L(tg3, st.get("bco_r")), inc=sV)
                    bank_r[6] = tg4
                    tg5 = pb.op("scalar", lambda e: e.activation(lam, gcol, AF.Exp),
                                waits=L(tg2, st.get("lam_r")), inc=sA)
                    tg6 = pb.op("vector", lambda e: e.tensor_single_scalar(
                        nbc, bco, -1.0, ALU.mult), waits=L(tg4, st.get("nbc_r")), inc=sV)

                    for h in range(4):
                        tr1 = pb.op("tensor", lambda e, h=h: e.transpose(
                            p32[0:1, 6, 0:128], gcol[:, h:h+1], sident32),
                            waits=L(tg2, bank_r[6]), inc=sT)
                        bank_w[6] = tr1
                        tr2 = pb.op("vector", lambda e: e.tensor_copy(
                            scr[0:1, 512:640], p32[0:1, 6, 0:128]),
                            waits=L(tr1, st.get("grow_r")), inc=sV)
                        bank_r[6] = tr2
                        tr3 = pb.op("tensor", lambda e: e.matmul(
                            p32[:, 6, 0:1], sones_f32, scr[0:1, 639:640],
                            start=True, stop=True), waits=L(tr2, t_of, bank_r[6]), inc=sT)
                        bank_w[6] = tr3
                        tr4 = pb.op("vector", lambda e: e.tensor_copy(gcc, p32[:, 6, 0:1]),
                                    waits=L(tr3, st.get("gcc_r")), inc=sV)
                        bank_r[6] = tr4
                        tr5 = pb.op("scalar", lambda e: e.activation(
                            lcc, gcc, AF.Exp), waits=L(tr4, st.get("lcc_r")), inc=sA)
                        tr6 = pb.op("vector", lambda e, h=h: e.tensor_tensor(
                            tc1, gcc, gcol[:, h:h+1], ALU.subtract), waits=L(tr4, tg2), inc=sV)
                        tr7 = pb.op("scalar", lambda e: e.activation(
                            khs, tc1, AF.Exp), waits=L(tr6, st.get("khs_r")), inc=sA)
                        bkb = nextbank()
                        td1 = pb.op("tensor", lambda e, bkb=bkb: e.matmul(
                            p32[:, bkb, 0:128], sones_f32, scr[0:1, 512:640],
                            start=True, stop=True), waits=L(tr2, t_of, bank_r[bkb]), inc=sT)
                        bank_w[bkb] = td1
                        td2 = pb.op("vector", lambda e, h=h, bkb=bkb: e.tensor_scalar(
                            scr[:, 512:640], p32[:, bkb, 0:128], gcol[:, h:h+1], 0.0,
                            ALU.subtract, ALU.max),
                            waits=L(td1, st.get("grow_r2")), inc=sV)
                        bank_r[bkb] = td2
                        td3 = pb.op("scalar", lambda e: e.activation(
                            mats[:, M_DS, :], scr[:, 512:640], AF.Exp, scale=-1.0),
                            waits=L(td2, st.get("mDS_r")), inc=sA)
                        td4 = pb.op("vector", lambda e: e.tensor_tensor(
                            mats[:, M_DS, :], mats[:, M_DS, :], slts, ALU.mult),
                            waits=L(td3, t_cm), inc=sV)
                        td5 = pb.op("vector", lambda e, h=h, bkb=bkb: e.tensor_scalar(
                            scr[:, 512:640], p32[:, bkb, 0:128], gcol[:, h:h+1], 0.0,
                            ALU.subtract, ALU.min), waits=L(td2, td3), inc=sV)
                        st["grow_r2"] = td5
                        bank_r[bkb] = td5
                        td6 = pb.op("scalar", lambda e: e.activation(
                            mats[:, M_ET, :], scr[:, 512:640], AF.Exp, scale=1.0),
                            waits=L(td5, st.get("mET_r")), inc=sA)
                        td7 = pb.op("vector", lambda e: e.tensor_tensor(
                            mats[:, M_DST, :], mats[:, M_ET, :], suts, ALU.mult),
                            waits=L(td6, t_cm, st.get("mDST_r")), inc=sV)
                        td8 = pb.op("vector", lambda e: e.tensor_tensor(
                            mats[:, M_ET, :], mats[:, M_ET, :], suti, ALU.mult),
                            waits=[td7], inc=sV)
                        st["grow_r"] = td6
                        st["gcol_rX"] = td5
                        bkb2 = nextbank()
                        tk1 = None
                        for cc in range(2):
                            tk1 = pb.op("tensor", lambda e, cc=cc, h=h, ch=ch, bkb2=bkb2:
                                e.matmul(
                                    p32[:, bkb2, 0:128], kp[:, 2*h+cc, 3+ch*128:3+(ch+1)*128],
                                    kp[:, 2*h+cc, 3+ch*128:3+(ch+1)*128],
                                    start=(cc == 0), stop=(cc == 1)),
                                waits=L(t_norm, bank_r[bkb2]) if cc == 0 else [], inc=sT)
                        bank_w[bkb2] = tk1
                        tn1 = pb.op("vector", lambda e, h=h, bkb2=bkb2: e.scalar_tensor_tensor(
                            mats[:, M_N, :], p32[:, bkb2, 0:128], nbc[:, h:h+1],
                            mats[:, M_DS, :], ALU.mult, ALU.mult),
                            waits=L(tk1, td4, tg6, st.get("mN_r")), inc=sV)
                        bkb3 = nextbank()
                        tb1 = pb.op("tensor", lambda e, h=h: e.transpose(
                            p32[0:1, 6, 0:128], nbc[:, h:h+1], sident32),
                            waits=L(tg6, bank_r[6]), inc=sT)
                        bank_w[6] = tb1
                        tb2 = pb.op("vector", lambda e: e.tensor_copy(
                            scr[0:1, 640:768], p32[0:1, 6, 0:128]),
                            waits=L(tb1, st.get("nbrow_r")), inc=sV)
                        bank_r[6] = tb2
                        tb3 = pb.op("tensor", lambda e, bkb3=bkb3: e.matmul(
                            p32[:, bkb3, 0:128], sones_f32, scr[0:1, 640:768],
                            start=True, stop=True), waits=L(tb2, t_of, bank_r[bkb3]), inc=sT)
                        bank_w[bkb3] = tb3
                        st["nbrow_r"] = tb3
                        tn2 = pb.op("vector", lambda e, bkb2=bkb2: e.tensor_tensor(
                            mats[:, M_NT, :], p32[:, bkb2, 0:128], mats[:, M_DST, :],
                            ALU.mult), waits=L(tk1, td7, st.get("mNT_r")), inc=sV)
                        bank_r[bkb2] = tn2
                        tn3 = pb.op("vector", lambda e, bkb3=bkb3: e.tensor_tensor(
                            mats[:, M_NT, :], mats[:, M_NT, :], p32[:, bkb3, 0:128],
                            ALU.mult), waits=[tn2, tb3], inc=sV)
                        bank_r[bkb3] = tn3
                        st["mDST_r"] = tn2
                        tp1 = pb.op("vector", lambda e: e.tensor_tensor(
                            mats[:, M_P, :], mats[:, M_N, :], sident, ALU.add),
                            waits=L(tn1, t_id, st.get("mP_r")), inc=sV)
                        tp2 = pb.op("vector", lambda e: e.tensor_tensor(
                            mats[:, M_PT, :], mats[:, M_NT, :], sident, ALU.add),
                            waits=L(tn3, st.get("mPT_r")), inc=sV)
                        sX, sXT = M_N, M_NT
                        tX, tXT = tn1, tn3
                        tP, tPT = tp1, tp2
                        for lv in range(6):
                            bka = nextbank()
                            q1 = pb.op("tensor", lambda e, sX=sX, sXT=sXT, bka=bka: e.matmul(
                                p32[:, bka, 0:128], mats[:, sXT, :], mats[:, sX, :],
                                start=True, stop=True),
                                waits=L(tX, tXT, bank_r[bka]), inc=sT)
                            bank_w[bka] = q1
                            bkb4 = nextbank()
                            q2 = pb.op("tensor", lambda e, sX=sX, sXT=sXT, bkb4=bkb4: e.matmul(
                                p32[:, bkb4, 0:128], mats[:, sX, :], mats[:, sXT, :],
                                start=True, stop=True),
                                waits=L(tX, tXT, bank_r[bkb4]), inc=sT)
                            bank_w[bkb4] = q2
                            nX, nXT = (M_XB, M_XBT) if sX == M_N else (M_N, M_NT)
                            e1 = pb.op("vector", lambda e, nX=nX, bka=bka: e.tensor_copy(
                                mats[:, nX, :], p32[:, bka, 0:128]),
                                waits=L(q1, st.get(("mx_r", nX))), inc=sV)
                            bank_r[bka] = e1
                            e2 = pb.op("vector", lambda e, nXT=nXT, bkb4=bkb4: e.tensor_copy(
                                mats[:, nXT, :], p32[:, bkb4, 0:128]),
                                waits=L(q2, st.get(("mx_r", nXT))), inc=sV)
                            bank_r[bkb4] = e2
                            bkc = nextbank()
                            q3 = pb.op("tensor", lambda e, nXT=nXT, bkc=bkc: e.matmul(
                                p32[:, bkc, 0:128], mats[:, nXT, :], mats[:, M_P, :],
                                start=True, stop=True), waits=L(e2, tP, bank_r[bkc]), inc=sT)
                            bank_w[bkc] = q3
                            bkd = nextbank()
                            q4 = pb.op("tensor", lambda e, nXT=nXT, bkd=bkd: e.matmul(
                                p32[:, bkd, 0:128], mats[:, M_P, :]
                                if False else mats[:, M_P, :], mats[:, nXT, :],
                                start=True, stop=True), waits=L(e1, e2, tPT, bank_r[bkd]),
                                inc=sT)
                            bank_w[bkd] = q4
                            e3 = pb.op("vector", lambda e, bkc=bkc: e.tensor_tensor(
                                mats[:, M_P, :], mats[:, M_P, :], p32[:, bkc, 0:128],
                                ALU.add), waits=L(q3, q4, st.get("mP_r")), inc=sV)
                            bank_r[bkc] = e3
                            e4 = pb.op("vector", lambda e, bkd=bkd: e.tensor_tensor(
                                mats[:, M_PT, :], mats[:, M_PT, :], p32[:, bkd, 0:128],
                                ALU.add), waits=L(q4, st.get("mPT_r")), inc=sV)
                            bank_r[bkd] = e4
                            st[("mx_r", sX)] = q1
                            st[("mx_r", sXT)] = q2
                            sX, sXT = nX, nXT
                            tX, tXT = e1, e2
                            tP, tPT = e3, e4
                        st["mN_r"] = q1
                        st["mNT_r"] = q2
                        st["mP_r"] = tP
                        st["mPT_r"] = tPT
                        tt1 = pb.op("vector", lambda e, h=h: e.tensor_tensor(
                            tc1, nbc[:, h:h+1], lam[:, h:h+1], ALU.mult),
                            waits=L(tg5, tg6, tr6), inc=sV)
                        tt2 = pb.op("tensor", lambda e: e.transpose(
                            p32[0:1, 6, 0:128], tc1, sident32),
                            waits=L(tt1, bank_r[6], tr3), inc=sT)
                        bank_w[6] = tt2
                        tt3 = pb.op("vector", lambda e: e.tensor_copy(
                            scr[0:1, 640:768], p32[0:1, 6, 0:128]),
                            waits=L(tt2, st.get("nbrow_r")), inc=sV)
                        bank_r[6] = tt3
                        bkb5 = nextbank()
                        tt4 = pb.op("tensor", lambda e, bkb5=bkb5: e.matmul(
                            p32[:, bkb5, 0:128], sones_f32, scr[0:1, 640:768],
                            start=True, stop=True), waits=L(tt3, t_of, bank_r[bkb5]), inc=sT)
                        bank_w[bkb5] = tt4
                        st["nbrow_r"] = tt4
                        tt5 = []
                        for cc in range(2):
                            tt5.append(pb.op("vector", lambda e, cc=cc, h=h, ch=ch, bkb5=bkb5:
                                e.tensor_tensor(
                                    mats[:, M_SK0+cc, :],
                                    kp[:, 2*h+cc, 3+ch*128:3+(ch+1)*128],
                                    p32[:, bkb5, 0:128], ALU.mult),
                                waits=L(tt4, t_norm, st.get(("msk_r", cc))), inc=sV))
                        bank_r[bkb5] = tt5[-1]
                        tB = None
                        for cc in range(2):
                            tB = pb.op("tensor", lambda e, cc=cc, h=h: e.matmul(
                                p32[:, 1, :], mats[:, M_SK0+cc, :], Sst[:, h*2+cc, :],
                                start=(cc == 0), stop=(cc == 1)),
                                waits=L(tt5[cc], tS0, st.get("S_w"), bank_r[1])
                                if cc == 0 else [tt5[cc]], inc=sT)
                        bank_w[1] = tB
                        tBe = pb.op("vector", lambda e, h=h, ch=ch: e.scalar_tensor_tensor(
                            bd[:, 0, :], vtk[:, h*512:(h+1)*512], bco[:, h:h+1],
                            p32[:, 1, :], ALU.mult, ALU.add),
                            waits=L(tB, t_tok, tg4, st.get("bd0_r")), inc=sV)
                        bank_r[1] = tBe
                        tD = pb.op("tensor", lambda e: e.matmul(
                            p32[:, 1, :], mats[:, M_PT, :], bd[:, 0, :],
                            start=True, stop=True), waits=L(tPT, tBe, bank_r[1]), inc=sT)
                        bank_w[1] = tD
                        tDe = pb.op("vector", lambda e: e.tensor_copy(
                            bd[:, 1, :], p32[:, 1, :]), waits=L(tD, st.get("bd1_r")), inc=sV)
                        bank_r[1] = tDe
                        st["bd0_r"] = tD
                        tql = pb.op("tensor", lambda e, h=h: e.transpose(
                            p32[0:1, 6, 0:128], lam[:, h:h+1], sident32),
                            waits=L(tg5, bank_r[6], tt4), inc=sT)
                        bank_w[6] = tql
                        tq2_ = pb.op("vector", lambda e: e.tensor_copy(
                            scr[0:1, 640:768], p32[0:1, 6, 0:128]),
                            waits=L(tql, st.get("nbrow_r")), inc=sV)
                        bank_r[6] = tq2_
                        bkb6 = nextbank()
                        tq3_ = pb.op("tensor", lambda e, bkb6=bkb6: e.matmul(
                            p32[:, bkb6, 0:128], sones_f32, scr[0:1, 640:768],
                            start=True, stop=True), waits=L(tq2_, t_of, bank_r[bkb6]),
                            inc=sT)
                        bank_w[bkb6] = tq3_
                        st["nbrow_r"] = tq3_
                        tq4_ = []
                        for cc in range(2):
                            tq4_.append(pb.op("vector", lambda e, cc=cc, h=h, ch=ch, bkb6=bkb6:
                                e.tensor_tensor(
                                    mats[:, M_SK0+cc, :],
                                    qp[:, 2*h+cc, 3+ch*128:3+(ch+1)*128],
                                    p32[:, bkb6, 0:128], ALU.mult),
                                waits=L(tq3_, tB, st.get(("msk_r", cc))), inc=sV))
                        bank_r[bkb6] = tq4_[-1]
                        st[("msk_r", 0)] = tq4_[0]
                        st[("msk_r", 1)] = tq4_[1]
                        tO = None
                        for cc in range(2):
                            tO = pb.op("tensor", lambda e, cc=cc, h=h: e.matmul(
                                p32[:, 0, :], mats[:, M_SK0+cc, :], Sst[:, h*2+cc, :],
                                start=(cc == 0), stop=False),
                                waits=L(tq4_[cc], bank_r[0], st.get("S_w"))
                                if cc == 0 else [tq4_[cc]], inc=sT)
                        bkb7 = nextbank()
                        tA = None
                        for cc in range(2):
                            tA = pb.op("tensor", lambda e, cc=cc, h=h, ch=ch, bkb7=bkb7:
                                e.matmul(
                                    p32[:, bkb7, 0:128],
                                    kp[:, 2*h+cc, 3+ch*128:3+(ch+1)*128],
                                    qp[:, 2*h+cc, 3+ch*128:3+(ch+1)*128],
                                    start=(cc == 0), stop=(cc == 1)),
                                waits=L(t_norm, bank_r[bkb7], tq4_[-1]) if cc == 0 else [],
                                inc=sT)
                        bank_w[bkb7] = tA
                        tAe = pb.op("vector", lambda e, bkb7=bkb7: e.tensor_tensor(
                            mats[:, M_ET, :], mats[:, M_ET, :], p32[:, bkb7, 0:128],
                            ALU.mult), waits=L(tA, td8), inc=sV)
                        bank_r[bkb7] = tAe
                        st["mET_r"] = tAe
                        tO = pb.op("tensor", lambda e: e.matmul(
                            p32[:, 0, :], mats[:, M_ET, :], bd[:, 1, :],
                            start=False, stop=True), waits=L(tAe, tDe, tO), inc=sT)
                        bank_w[0] = tO
                        tkh = pb.op("vector", lambda e, h=h, ch=ch: e.tensor_scalar(
                            mats[:, M_SK0, :], kt[:, h*256:h*256+128], khs, None,
                            ALU.mult), waits=L(tr7, tke, tq4_[0], tO), inc=sV)
                        tkh2 = pb.op("vector", lambda e, h=h, ch=ch: e.tensor_scalar(
                            mats[:, M_SK1, :], kt[:, h*256+128:h*256+256], khs, None,
                            ALU.mult), waits=L(tr7, tke, tq4_[1]), inc=sV)
                        st[("msk_r", 0)] = tkh
                        st[("msk_r", 1)] = tkh2
                        st["khs_r"] = tkh2
                        tsu = []
                        for cc, tkh_ in ((0, tkh), (1, tkh2)):
                            tku = pb.op("tensor", lambda e, cc=cc: e.matmul(
                                p32[:, 2+cc, :], mats[:, M_SK0+cc, :], bd[:, 1, :],
                                start=True, stop=True),
                                waits=L(tkh_, tDe, bank_r[2+cc]), inc=sT)
                            bank_w[2+cc] = tku
                            tse = pb.op("vector", lambda e, cc=cc, h=h: e.scalar_tensor_tensor(
                                Sst[:, h*2+cc, :], Sst[:, h*2+cc, :], lcc,
                                p32[:, 2+cc, :], ALU.mult, ALU.add),
                                waits=L(tku, tr5, tO, tB), inc=sV)
                            bank_r[2+cc] = tse
                            tsu.append(tse)
                        st["S_w"] = tsu[-1]
                        st["lcc_r"] = tsu[-1]
                        st["gcc_r"] = tr5
                        st["bd1_r"] = tku
                        vo1 = pb.op("scalar", lambda e: e.activation(
                            garb[:, 0:512], p32[:, 0, :], AF.Square, accum_out=tc1),
                            waits=L(tO, st.get("garb_w"), st.get("ftile_r"), tt1), inc=sA)
                        st["garb_w"] = vo1
                        vo2 = pb.op("scalar", lambda e: e.activation(
                            tc1, tc1, AF.Ln, bias=ccol("rms_eps"), scale=1.0/512),
                            waits=[], wforce=[vo1], inc=sA)
                        vo3 = pb.op("scalar", lambda e: e.activation(
                            rocol, tc1, AF.Exp, scale=-0.5),
                            waits=L(vo2, st.get("rocol_r")), inc=sA)
                        tks = []
                        for vc in range(4):
                            tks.append(pb.op("tensor", lambda e, vc=vc, h=h, ch=ch:
                                e.transpose(ptr[:, vc*128:(vc+1)*128],
                                            gt[:, h*4+vc, ch*128:(ch+1)*128], sident),
                                waits=L(te2, st.get("ptr_r"), bank_r[7])
                                if vc == 0 else [], inc=sT))
                        st["ptr_w"] = tks[-1]
                        tge = pb.op("vector", lambda e: e.tensor_copy(
                            gtokv[:, 0:512], ptr[:, 0:512]),
                            waits=L(tks[-1], st.get("gtok_r")), inc=sV)
                        st["ptr_r"] = tge
                        vp1 = pb.op("vector", lambda e: e.scalar_tensor_tensor(
                            bd[:, 0, :], p32[:, 0, :], rocol, gtokv[:, 0:512],
                            ALU.mult, ALU.mult), waits=L(vo3, tge, tD), inc=sV)
                        bank_r[0] = vp1
                        st["gtok_r"] = vp1
                        st["rocol_r"] = vp1
                        st["bd0_r"] = vp1
                        tks = []
                        for vc in range(4):
                            tks.append(pb.op("tensor", lambda e, vc=vc: e.transpose(
                                ptr[:, vc*128:(vc+1)*128], bd[:, 0, vc*128:(vc+1)*128],
                                sident), waits=L(vp1, st.get("ptr_r"), bank_r[7])
                                if vc == 0 else [], inc=sT))
                        st["ptr_w"] = tks[-1]
                        tpe = pb.op("vector", lambda e, h=h, ch=ch: e.tensor_copy(
                            pf[:, h*4:(h+1)*4, ch*128:(ch+1)*128],
                            ptr[:, 0:512].rearrange("p (c t) -> p c t", c=4)),
                            waits=L(tks[-1], st.get("pf_r")), inc=sV)
                        st["ptr_r"] = tpe
                        st["bd0_r"] = tks[-1]
                    st["kt_r"] = tku
                    st["vt_r"] = tBe
                    st["gt_r"] = tks[-1]
                    st["gcol_r"] = tkh2
                    st["bco_r"] = tBe
                    st["lam_r"] = tq4_[-1]
                    st["nbc_r"] = tt5[-1]
                t_scan = tpe
                st["pf_w"] = t_scan

                tok_wo_cons = [st.get("wo0_r"), st.get("wo1_r")]
                tok_enc = []
                for oc in range(8):
                    par = oc % 2
                    tld = pb.dma("gpsimd", wo_[:, par, :, :],
                                 dr["a0_wo"][oc].rearrange("p (c t) -> p c t", c=16),
                                 waits=L(tok_wo_cons[par], *barrier), inc=SL["we" if par == 0 else "wf"])
                    for ts_ in range(4):
                        bke = nextbank()
                        tk = None
                        for fc in range(16):
                            tk = pb.op("tensor", lambda e, fc=fc, ts_=ts_, par=par, bke=bke:
                                e.matmul(
                                    p32[:, bke, 0:128], pf[:, fc, ts_*128:(ts_+1)*128],
                                    wo_[:, par, fc, :], start=(fc == 0), stop=(fc == 15)),
                                waits=L(tld, t_scan, bank_r[bke]) if fc == 0 else [],
                                inc=sT)
                        bank_w[bke] = tk
                        tok_wo_cons[par] = tk
                        tev = pb.op("vector", lambda e, ts_=ts_, oc=oc, bke=bke:
                            e.tensor_tensor(
                                xblk[:, ts_, oc*128:(oc+1)*128],
                                xblk[:, ts_, oc*128:(oc+1)*128],
                                p32[:, bke, 0:128], ALU.add),
                            waits=L(tk, txb, st.get("xblk_w")), inc=sV)
                        bank_r[bke] = tev
                        st["xblk_w"] = tev
                        tok_enc.append(tk)
                for ts_ in range(4):
                    tile = base + sc*4 + ts_
                    tc, trs, tsq = ln_tile(xblk[:, ts_, :], ccol("ln_eps"), garb,
                                           L(st.get("xblk_w"), st.get("ftile_r")))
                    tf1 = pb.op("vector", lambda e: e.tensor_scalar(
                        ctile, ctile, scr[:, SB+1:SB+2], None, ALU.mult),
                        waits=L(tc, trs, tsq, st.get("ctile_r")), inc=sV)
                    st["rstd_r"] = tf1
                    tf2 = pb.op("vector", lambda e: e.tensor_tensor(
                        ctile, ctile, lnv[:, 2, :], ALU.mult), waits=[tf1], inc=sV)
                    tf3 = pb.op("vector", lambda e: e.tensor_tensor(
                        ctile, ctile, lnv[:, 3, :], ALU.add), waits=[tf2], inc=sV)
                    tf4 = pb.op("vector", lambda e: e.tensor_copy(ftile, ctile),
                                waits=L(tf3, st.get("ftile_r"), st.get("garb_w")), inc=sV)
                    st["ctile_r"] = tf4
                    st["garb_w"] = tf4
                    ttr, tev = transpose_tile_to(hT[:, :, ts_*128:(ts_+1)*128], ftile, 8,
                                                 [tf4, tok_enc[-1], st.get("hT_r")])
                    st["ftile_r"] = ttr
                    tok_w2_cons = tok_wo_cons
                    for oc in range(8):
                        par = oc % 2
                        tld = pb.dma("gpsimd", wo_[:, par, 0:8, :],
                                     dr["a0_w2"][oc].rearrange("p (c t) -> p c t", c=8),
                                     waits=L(tok_w2_cons[par], *barrier),
                                     inc=SL["we" if par == 0 else "wf"])
                        bke = nextbank()
                        tk = None
                        for dd in range(8):
                            tk = pb.op("tensor", lambda e, dd=dd, ts_=ts_, par=par, bke=bke:
                                e.matmul(
                                    p32[:, bke, 0:128], hT[:, dd, ts_*128:(ts_+1)*128],
                                    wo_[:, par, dd, :], start=(dd == 0), stop=(dd == 7)),
                                waits=L(tld, tev, bank_r[bke]) if dd == 0 else [], inc=sT)
                        bank_w[bke] = tk
                        tok_w2_cons[par] = tk
                        tok_wo_cons[par] = tk
                        g1 = pb.op("vector", lambda e, oc=oc, bke=bke: e.tensor_tensor(
                            scr[:, 512:640], p32[:, bke, 0:128],
                            lnv[:, 5, oc*128:(oc+1)*128], ALU.add),
                            waits=L(tk, t_lnv, st.get("grow_r"), st.get("grow_r2")), inc=sV)
                        g2 = pb.op("scalar", lambda e: e.activation(
                            garb[:, 0:128], scr[:, 512:640], AF.Gelu),
                            waits=L(g1, st.get("garb_w")), inc=sA)
                        st["garb_w"] = g2
                        g3 = pb.op("vector", lambda e, ts_=ts_, oc=oc: e.tensor_tensor(
                            xblk[:, ts_, oc*128:(oc+1)*128],
                            xblk[:, ts_, oc*128:(oc+1)*128],
                            garb[:, 0:128], ALU.add), waits=L(g2, st.get("xblk_w")), inc=sV)
                        bank_r[bke] = g3
                        st["xblk_w"] = g3
                        st["ftile_r"] = g3
                        st["grow_r"] = g3
                    tfin_ = pb.op("vector", lambda e, tile=tile, ts_=ts_: e.tensor_tensor(
                        xres[:, tile, :], xres[:, tile, :], xblk[:, ts_, :], ALU.add),
                        waits=L(st.get("xblk_w"), xw(tile), xr(tile)), inc=sV)
                    set_xw(tile, tfin_)
                    st["xblk_r"] = tfin_
                st["wo0_r"] = tok_wo_cons[0]
                st["wo1_r"] = tok_wo_cons[1]
                st["hT_r"] = tk
                st["q_pre_r"] = t_conv
                st["k_pre_r"] = tku
                st["v_pre_r"] = tBe
            st["S_r"] = st["S_w"]
            st["gsc_r"] = tkh2
            st["wp0_r"] = tok_wp_cons[0]
            st["wp1_r"] = tok_wp_cons[1]
        st["lnv_r"] = (sV, sV.n)
        st["pf_r"] = tok_enc[-1]

    def emit_attn1():
        barrier = snap()
        o = 0
        o_kf, o = o, o + 16384
        o_vt, o = o, o + 16384
        o_wA, o = o, o + 8192
        o_wB, o = o, o + 8192
        o_xt, o = o, o + 1024
        o_qf, o = o, o + 1024
        o_pt, o = o, o + 2048
        o_pw, o = o, o + 2048
        o_yt, o = o, o + 1024
        o_ft, o = o, o + 1024
        o_bvb, o = o, o + 1024
        o_bob, o = o, o + 1024
        assert o <= ARENA_N, o
        kf = arena[:, o_kf:o_kf+16384].rearrange("p (c t) -> p c t", c=8)
        vt = arena[:, o_vt:o_vt+16384].rearrange("p (c t) -> p c t", c=32)
        wA = arena[:, o_wA:o_wA+8192].rearrange("p (c t) -> p c t", c=8)
        wB = arena[:, o_wB:o_wB+8192].rearrange("p (c t) -> p c t", c=8)
        xt = arena[:, o_xt:o_xt+1024].rearrange("p (c t) -> p c t", c=8)
        qf = arena[:, o_qf:o_qf+1024].rearrange("p (c t) -> p c t", c=8)
        ptb = arena[:, o_pt:o_pt+2048].rearrange("p (c t) -> p c t", c=16)
        prow = arena[:, o_pw:o_pw+2048]
        yt = arena[:, o_yt:o_yt+1024].rearrange("p (c t) -> p c t", c=8)
        ftile = arena[:, o_ft:o_ft+1024]
        bvb = arena[:, o_bvb:o_bvb+1024]
        bob = arena[:, o_bob:o_bob+1024]

        Ssb = scr[:, 0:2048]
        c0 = 2048
        o_bq, c0 = c0, c0 + 8
        o_bk, c0 = c0, c0 + 8
        o_rmax, c0 = c0, c0 + 4
        assert c0 <= 2560
        bqsb = scr[:, o_bq:o_bq+8]
        bksb = scr[:, o_bk:o_bk+8]
        rmax = scr[:, o_rmax:o_rmax+1]
        nmax = scr[:, o_rmax+1:o_rmax+2]
        rsum = scr[:, o_rmax+2:o_rmax+3]
        rrec = scr[:, o_rmax+3:o_rmax+4]

        bq3 = dr["a1_bq"].rearrange("(c p) -> p c", p=128)
        bk3 = dr["a1_bk"].rearrange("(c p) -> p c", p=128)
        t_bq = pb.dma("sync", bqsb, bq3, waits=barrier, inc=sC, slow=True)
        t_bk = pb.dma("sync", bksb, bk3, inc=sC, slow=True)
        t_bq = t_bk = (sC, sC.n)
        tq32 = pb.op("vector", lambda e: e.tensor_single_scalar(bqsb, bqsb, 1.0/32.0, ALU.mult),
                     waits=L(t_bq, *barrier), inc=sV)
        tbm = None
        for nm, dst in (("a1_bv", bvb), ("a1_bo", bob)):
            for hh in range(2):
                trow = pb.dma("sync", scr[0:1, 1024:1536], dr[nm][None, hh*512:(hh+1)*512],
                              waits=L(tbm, st.get("ctile_r")), inc=sC)
                tbm = pb.op("tensor", lambda e: e.matmul(
                    p32[:, 6, :], sones_f32, scr[0:1, 1024:1536], start=True, stop=True),
                    waits=L(t_of, trow, bank_r[6], *barrier), inc=sT)
                bank_w[6] = tbm
                tbe = pb.op("vector", lambda e, dst=dst, hh=hh: e.tensor_copy(
                    dst[:, hh*512:(hh+1)*512], p32[:, 6, :]), waits=[tbm], inc=sV)
                bank_r[6] = tbe
        t_bvb = (sV, sV.n)

        for sq in range(SEQ_PER_CORE):
            base = sq * 16
            wk3 = dr["a1_wk"].rearrange("p (c n) -> p c n", c=8)
            wv3 = dr["a1_wv"].rearrange("p (c n) -> p c n", c=8)
            t_wA = pb.dma("gpsimd", wA[:, :, :], wk3, waits=L(st.get("wA_r"), *barrier),
                          inc=SL["wa"])
            t_wB = pb.dma("gpsimd", wB[:, :, :], wv3, waits=L(st.get("wB_r"), *barrier),
                          inc=SL["wb"])
            for j in range(16):
                tile = base + j
                tc, trs, tsq = ln_tile(xres[:, tile, :], ccol("ln_eps"), ftile,
                                       L(xw(tile), st.get("ftile_r"), *barrier))
                set_xr(tile, tc)
                tf = pb.op("vector", lambda e: e.tensor_scalar(
                    ftile, ctile, scr[:, SB+1:SB+2], None, ALU.mult),
                    waits=L(tc, trs, tsq, st.get("ftile_r")), inc=sV)
                st["rstd_r"] = tf
                ttr, tev = transpose_tile_to(xt[:, :, :], ftile, 8, [tf])
                st["ftile_r"] = ttr
                for oc in range(8):
                    tk = None
                    for dd in range(8):
                        tk = pb.op("tensor", lambda e, dd=dd, oc=oc: e.matmul(
                            p32[:, 4, :][:, 0:128],
                            wA[:, dd, oc*128:(oc+1)*128], xt[:, dd, :],
                            start=(dd == 0), stop=(dd == 7)),
                            waits=L(t_wA, tev, bank_r[4]) if dd == 0 else [], inc=sT)
                    bank_w[4] = tk
                    tke = pb.op("scalar", lambda e, oc=oc, j=j: e.activation(
                        kf[:, oc, j*128:(j+1)*128], p32[:, 4, 0:128], AF.Relu,
                        bias=bksb[:, oc:oc+1]),
                        waits=L(tk, t_bk, st.get("kf_r")), inc=sA)
                    bank_r[4] = tke
                for hh in range(2):
                    tk = None
                    for dd in range(8):
                        tk = pb.op("tensor", lambda e, dd=dd, hh=hh: e.matmul(
                            p32[:, 5, :], xt[:, dd, :], wB[:, dd, hh*512:(hh+1)*512],
                            start=(dd == 0), stop=(dd == 7)),
                            waits=L(t_wB, tev, bank_r[5]) if dd == 0 else [], inc=sT)
                    bank_w[5] = tk
                    tv1 = pb.op("vector", lambda e, j=j, hh=hh: e.tensor_tensor(
                        vt[:, 2*j+hh, :], p32[:, 5, :], bvb[:, hh*512:(hh+1)*512], ALU.add),
                        waits=L(tk, t_bvb, st.get("vt_r")), inc=sV)
                    tv2 = pb.op("vector", lambda e, j=j, hh=hh: e.tensor_scalar_max(
                        vt[:, 2*j+hh, :], vt[:, 2*j+hh, :], 0.0), waits=[tv1], inc=sV)
                    bank_r[5] = tv2
            tok_kv_done = (sT, sT.n)
            wq3 = dr["a1_wq"].rearrange("p (c n) -> p c n", c=8)
            wo3 = dr["a1_wo"].rearrange("p (c n) -> p c n", c=8)
            t_wA = pb.dma("gpsimd", wA[:, :, :], wq3, waits=L(tok_kv_done, *barrier), inc=SL["wa"])
            t_wB = pb.dma("gpsimd", wB[:, :, :], wo3, waits=L(tok_kv_done, *barrier), inc=SL["wb"])
            for j in range(16):
                tile = base + j
                tc, trs, tsq = ln_tile(xres[:, tile, :], ccol("ln_eps"), ftile,
                                       L(xw(tile), st.get("ftile_r"), st.get("Ssb_r")))
                set_xr(tile, tc)
                tf = pb.op("vector", lambda e: e.tensor_scalar(
                    ftile, ctile, scr[:, SB+1:SB+2], None, ALU.mult),
                    waits=L(tc, trs, tsq, st.get("ftile_r")), inc=sV)
                st["rstd_r"] = tf
                ttr, tev = transpose_tile_to(xt[:, :, :], ftile, 8, [tf])
                st["ftile_r"] = ttr
                for oc in range(8):
                    tk = None
                    for dd in range(8):
                        tk = pb.op("tensor", lambda e, dd=dd, oc=oc: e.matmul(
                            p32[:, 4, 0:128],
                            wA[:, dd, oc*128:(oc+1)*128], xt[:, dd, :],
                            start=(dd == 0), stop=(dd == 7)),
                            waits=L(t_wA, tev, bank_r[4]) if dd == 0 else [], inc=sT)
                    bank_w[4] = tk
                    tqe = pb.op("scalar", lambda e, oc=oc: e.activation(
                        qf[:, oc, :], p32[:, 4, 0:128], AF.Relu,
                        bias=bqsb[:, oc:oc+1], scale=1.0/32.0),
                        waits=L(tk, tq32, st.get("qf_r")), inc=sA)
                    bank_r[4] = tqe
                tok_q = tqe
                tok_sc = None
                for kb in range(j+1):
                    bk = 2 + (kb % 2)
                    tk = None
                    for dd in range(8):
                        tk = pb.op("tensor", lambda e, dd=dd, kb=kb, bk=bk: e.matmul(
                            p32[:, bk, 0:128], qf[:, dd, :], kf[:, dd, kb*128:(kb+1)*128],
                            start=(dd == 0), stop=(dd == 7)),
                            waits=L(tok_q, bank_r[bk], tke) if dd == 0 else [], inc=sT)
                    bank_w[bk] = tk
                    if kb == j:
                        tse = pb.op("vector", lambda e, kb=kb, bk=bk: e.tensor_tensor(
                            Ssb[:, kb*128:(kb+1)*128], p32[:, bk, 0:128], scmask,
                            ALU.subtract),
                            waits=L(tk, t_cm, st.get("ctile_r"), tf), inc=sV)
                    else:
                        tse = pb.op("vector", lambda e, kb=kb, bk=bk: e.tensor_copy(
                            Ssb[:, kb*128:(kb+1)*128], p32[:, bk, 0:128]),
                            waits=L(tk, st.get("ctile_r"), tf) if kb == 0 else [tk],
                            inc=sV)
                    bank_r[bk] = tse
                    tok_sc = tse
                KWID = (j+1)*128
                tm1 = pb.op("vector", lambda e, KWID=KWID: e.reduce_max(
                    rmax, Ssb[:, 0:KWID], AX.X), waits=L(tok_sc, st.get("rmax_r")), inc=sV)
                tm2 = pb.op("vector", lambda e: e.tensor_single_scalar(
                    nmax, rmax, -1.0, ALU.mult), waits=[tm1], inc=sV)
                tm3 = pb.op("scalar", lambda e, KWID=KWID: e.activation(
                    prow[:, 0:KWID], Ssb[:, 0:KWID], AF.Exp, bias=nmax,
                    accum_out=rsum),
                    waits=L(tm2, st.get("prow_r"), st.get("rsum_r")), inc=sA)
                st["rmax_r"] = tm3
                st["Ssb_r"] = tm3
                tm4 = pb.op("vector", lambda e: e.reciprocal(rrec, rsum), waits=[tm3], inc=sV)
                st["rsum_r"] = tm4
                tok_pt = None
                for kb in range(j+1):
                    ttp = pb.op("tensor", lambda e, kb=kb: e.transpose(
                        ptr[:, 0:128], prow[:, kb*128:(kb+1)*128], sident),
                        waits=L(tm3, st.get("ptr_r"), bank_r[7], t_id), inc=sT)
                    st["ptr_w"] = ttp
                    tpe = pb.op("vector", lambda e, kb=kb: e.tensor_copy(
                        ptb[:, kb, :], ptr[:, 0:128]),
                        waits=L(ttp, st.get("ptb_r")), inc=sV)
                    st["ptr_r"] = tpe
                    tok_pt = tpe
                st["prow_r"] = st["ptr_w"]
                last_pv = [None, None]
                for kb in range(j+1):
                    for hh in range(2):
                        tk = pb.op("tensor", lambda e, kb=kb, hh=hh, j=j: e.matmul(
                            p32[:, hh, :], ptb[:, kb, :], vt[:, 2*kb+hh, :],
                            start=(kb == 0), stop=(kb == j)),
                            waits=L(tok_pt, bank_r[hh], tv2) if kb == 0 else [], inc=sT)
                        last_pv[hh] = tk
                bank_w[0], bank_w[1] = last_pv
                tye = []
                for hh in range(2):
                    tye.append(pb.op("vector", lambda e, hh=hh: e.tensor_scalar(
                        ftile[:, hh*512:(hh+1)*512], p32[:, hh, :], rrec, None, ALU.mult),
                        waits=L(last_pv[hh], tm4, st.get("ftile_r")), inc=sV))
                bank_r[0], bank_r[1] = tye
                st["garb_w"] = tye[-1]
                ttr2, tev2 = transpose_tile_to(yt[:, :, :], ftile, 8, [tye[0], tye[1]])
                st["ftile_r"] = ttr2
                for hh in range(2):
                    tk = None
                    for dd in range(8):
                        tk = pb.op("tensor", lambda e, dd=dd, hh=hh: e.matmul(
                            p32[:, 4+hh, :], yt[:, dd, :], wB[:, dd, hh*512:(hh+1)*512],
                            start=(dd == 0), stop=(dd == 7)),
                            waits=L(t_wB, tev2, bank_r[4+hh]) if dd == 0 else [], inc=sT)
                    bank_w[4+hh] = tk
                    w1_ = pb.op("vector", lambda e, hh=hh: e.tensor_tensor(
                        ctile[:, hh*512:(hh+1)*512], p32[:, 4+hh, :],
                        bob[:, hh*512:(hh+1)*512], ALU.add),
                        waits=L(tk, t_bvb, st.get("ctile_r")), inc=sV)
                    w2_ = pb.op("vector", lambda e, hh=hh: e.tensor_scalar_max(
                        ctile[:, hh*512:(hh+1)*512], ctile[:, hh*512:(hh+1)*512], 0.0),
                        waits=[w1_], inc=sV)
                    w3_ = pb.op("vector", lambda e, hh=hh, tile=tile: e.tensor_tensor(
                        xres[:, tile, hh*512:(hh+1)*512],
                        xres[:, tile, hh*512:(hh+1)*512],
                        ctile[:, hh*512:(hh+1)*512], ALU.add),
                        waits=L(w2_, xw(tile), xr(tile)), inc=sV)
                    bank_r[4+hh] = w3_
                    set_xw(tile, w3_)
                    st["ctile_r"] = w3_
                st["qf_r"] = tk
                st["ptb_r"] = last_pv[1]
                st["vt_r"] = last_pv[1]
                st["kf_r"] = tok_sc
            st["wA_r"] = (sT, sT.n)
            st["wB_r"] = (sT, sT.n)

    for ph in phases:
        if ph in ("ffn0", "ffn1"):
            emit_ffn(0)
        elif ph in ("ffn2", "ffn3"):
            emit_ffn(1)
        elif ph == "attn0":
            emit_attn0()
        elif ph == "attn1":
            emit_attn1()

    import os as _os
    if _os.environ.get("DBG_NOSTORE"):
        with nc.Block() as block:
            pb.emit(block)
        return nc
    sbar = snap()
    yout3 = yout.rearrange("(t p) d -> p t d", p=128)
    for tile in range(NTILE):
        par = tile % 2
        stg = scr[:, par*1024:(par+1)*1024]
        tcv = pb.op("vector", lambda e, tile=tile, stg=stg: e.tensor_copy(
            stg, xres[:, tile, :]),
            waits=L(xw(tile), st.get(("stg_r", par)), *sbar), inc=sV)
        tst = pb.dma("sync", yout3[:, tile, :], stg, waits=[tcv], inc=sY)
        st[("stg_r", par)] = tst
    pb.op("sync", lambda e: e.wait_ge(sY.h, sY.n), waits=[])

    with nc.Block() as block:
        pb.emit(block)
    return nc


def _ptile_w1(w):
    w = np.asarray(w, np.float32).reshape(NE, 8, 128, 32, 128)
    return np.ascontiguousarray(w.transpose(0, 3, 2, 1, 4).reshape(NE, 32, 128, 1024))


def _ptile_w2(w):
    w = np.asarray(w, np.float32).reshape(NE, 8, 4, 128, 2, 512)
    return np.ascontiguousarray(w.transpose(0, 4, 1, 3, 2, 5).reshape(NE, 2, 8, 128, 2048))


def _ptile_proj(w):
    dout = w.shape[1]
    noc = dout // 128
    w = np.asarray(w, np.float32).reshape(8, 128, noc, 128)
    return np.ascontiguousarray(w.transpose(2, 1, 0, 3).reshape(noc, 128, 1024))


def _ptile_proj2(w):
    dout = w.shape[1]
    noc = dout // 128
    w = np.asarray(w, np.float32).reshape(16, 128, noc, 128)
    return np.ascontiguousarray(w.transpose(2, 1, 0, 3).reshape(noc, 128, 2048))


def _ptile_ab(wa, wb):
    out = np.zeros((128, 64), np.float32)
    for wi, w in enumerate((wa, wb)):
        w = np.asarray(w, np.float32).reshape(8, 128, 4)
        out[:, wi*32:(wi+1)*32] = w.transpose(1, 0, 2).reshape(128, 32)
    return np.ascontiguousarray(out)


def _ptile_full(w):
    w = np.asarray(w, np.float32).reshape(8, 128, 1024)
    return np.ascontiguousarray(w.transpose(1, 0, 2).reshape(128, 8192))


def _prep_inputs(inputs):
    f = np.ascontiguousarray
    m = {
        "a0_ln1_g": inputs["L0_attn_ln1_g"], "a0_ln1_b": inputs["L0_attn_ln1_b"],
        "a0_ln2_g": inputs["L0_attn_ln2_g"], "a0_ln2_b": inputs["L0_attn_ln2_b"],
        "a0_rms_w": inputs["L0_attn_rms_in_w"],
        "a0_wq": _ptile_proj(inputs["L0_attn_Wq"]),
        "a0_wk": _ptile_proj(inputs["L0_attn_Wk"]),
        "a0_wv": _ptile_proj(inputs["L0_attn_Wv"]),
        "a0_wat": _ptile_ab(inputs["L0_attn_Wa"], inputs["L0_attn_Wb"]),
        "a0_cq": inputs["L0_attn_conv_q_w"], "a0_ck": inputs["L0_attn_conv_k_w"],
        "a0_cv": inputs["L0_attn_conv_v_w"],
        "a0_wa": inputs["L0_attn_Wa"], "a0_dtb": inputs["L0_attn_dt_bias"],
        "a0_alog": inputs["L0_attn_A_log"], "a0_wb": inputs["L0_attn_Wb"],
        "a0_wg": _ptile_proj(inputs["L0_attn_Wg"]), "a0_onw": inputs["L0_attn_o_norm_w"],
        "a0_wo": _ptile_proj2(inputs["L0_attn_Wo"]), "a0_w2": _ptile_proj(inputs["L0_attn_W2"]),
        "a0_b2": inputs["L0_attn_b2"],
        "f0_wr": inputs["L0_ffn_Wr"], "f0_br": inputs["L0_ffn_br"],
        "f0_w1": _ptile_w1(inputs["L0_ffn_W1"]), "f0_w2": _ptile_w2(inputs["L0_ffn_W2"]),
        "a1_wq": _ptile_full(inputs["L1_attn_Wq"]), "a1_bq": inputs["L1_attn_bq"],
        "a1_wk": _ptile_full(inputs["L1_attn_Wk"]), "a1_bk": inputs["L1_attn_bk"],
        "a1_wv": _ptile_full(inputs["L1_attn_Wv"]), "a1_bv": inputs["L1_attn_bv"],
        "a1_wo": _ptile_full(inputs["L1_attn_Wo"]), "a1_bo": inputs["L1_attn_bo"],
        "f1_cent_t": f(np.asarray(inputs["L1_ffn_centroids"]).T),
        "f1_gam_t": f(np.asarray(inputs["L1_ffn_gamma"]).T),
        "f1_bet_t": f(np.asarray(inputs["L1_ffn_beta"]).T),
        "f1_w1t": _ptile_w1(np.transpose(np.asarray(inputs["L1_ffn_W1"]), (0, 2, 1))),
        "f1_b1t": f(np.asarray(inputs["L1_ffn_b1"]).T),
        "f1_w2t": _ptile_w2(np.transpose(np.asarray(inputs["L1_ffn_W2"]), (0, 2, 1))),
        "f1_b2": inputs["L1_ffn_b2"],
    }
    return {k: np.ascontiguousarray(np.asarray(v, dtype=np.float32)) for k, v in m.items()}


def kernel(**inputs):
    x = np.asarray(inputs["L0_attn_x"], dtype=np.float32)
    wmap = _prep_inputs(inputs)
    import os as _os
    _sm = _os.environ.get("STRICT_ENG", "vector")
    nc = build_nc(strict=(True if _sm == "all" else (set(_sm.split(",")) if _sm else False)))
    in_maps = []
    for c in range(NCORE):
        m = dict(wmap)
        m["x"] = np.ascontiguousarray(
            x[c*SEQ_PER_CORE:(c+1)*SEQ_PER_CORE].reshape(TOK, D))
        in_maps.append(m)
    res = run_bass_kernel_spmd(nc, in_maps, core_ids=list(range(NCORE)))
    out = np.concatenate(
        [np.asarray(r["y"]).reshape(SEQ_PER_CORE, T, D) for r in res.results], axis=0)
    return out.astype(np.float32)
